# version 01
# baseline (speedup 1.0000x reference)
import contextlib
import numpy as np

import concourse.bass as bass
import concourse.mybir as mybir
from concourse.bass_utils import run_bass_kernel_spmd

F32 = mybir.dt.float32
BF16 = mybir.dt.bfloat16
ALU = mybir.AluOpType
AF = mybir.ActivationFunctionType
AX = mybir.AxisListType

W = 8
L = 16384
TLOC = L // W
HALO = 3
TX = TLOC + HALO
H = 1024
E = 2048
NH = 32
P = 64
NS = 128
Q = 128
NCHUNK = TLOC // Q
FF = 4096
CONV = E + 2 * NS
DIN = 2 * E + 2 * NS + NH
DI = 2048
DS = 16
DSL = DI // W
TSUB = 512
NSUB = TLOC // TSUB
NSEC = L // TSUB
EPS0 = 1e-5
EPS1 = 1e-6
EPSM = 1e-5

ALL_STAGES = ("attn_a", "ffn1", "attn_b", "ffn2")


def _f(x):
    return np.ascontiguousarray(np.asarray(x), dtype=np.float32)


def prep_inputs(inp):
    x = _f(inp["L0_attn_a_hidden_states"])[0]
    xT = np.ascontiguousarray(x.T)
    w_inT = np.ascontiguousarray(_f(inp["L0_attn_a_w_in"]).T)
    w_outT = np.ascontiguousarray(_f(inp["L0_attn_a_w_out"]).T)
    w_gateT = np.ascontiguousarray(_f(inp["L0_attn_a_w_gate"]).T)
    w_upT = np.ascontiguousarray(_f(inp["L0_attn_a_w_up"]).T)
    w_downT = np.ascontiguousarray(_f(inp["L0_attn_a_w_down"]).T)
    ffw = _f(inp["L0_ffn_weight"])
    ffa1 = _f(inp["L0_ffn_aux_weight1"])
    ffa2 = _f(inp["L0_ffn_aux_weight2"])
    ffb = _f(inp["L0_ffn_bias"]).reshape(H, 1)
    w_ln1 = _f(inp["L0_attn_a_w_ln1"]).reshape(H, 1)
    w_ln2 = _f(inp["L0_attn_a_w_ln2"]).reshape(H, 1)
    w_mnorm = _f(inp["L0_attn_a_w_mnorm"]).reshape(E, 1)
    w_conv = _f(inp["L0_attn_a_w_conv"])
    b_conv = _f(inp["L0_attn_a_b_conv"]).reshape(CONV, 1)
    dt_bias = _f(inp["L0_attn_a_dt_bias"]).reshape(NH, 1)
    A_log = _f(inp["L0_attn_a_A_log"]).reshape(NH, 1)
    D_rep = _f(np.repeat(_f(inp["L0_attn_a_D"]), P).reshape(E, 1))
    b_WinT = np.ascontiguousarray(_f(inp["L0_attn_b_W_in"]).T)
    b_convw = _f(inp["L0_attn_b_conv_w"]).reshape(DI, 4)
    b_convb = _f(inp["L0_attn_b_conv_b"]).reshape(DI, 1)
    b_Alog = _f(inp["L0_attn_b_A_log"])
    b_WxT = np.ascontiguousarray(_f(inp["L0_attn_b_W_x"]).T)
    b_WoutT = np.ascontiguousarray(_f(inp["L0_attn_b_W_out"]).T)

    maps = []
    for k in range(W):
        t0 = k * TLOC
        if k == 0:
            xs = np.concatenate([np.zeros((H, HALO), np.float32), xT[:, :TLOC]], axis=1)
        else:
            xs = xT[:, t0 - HALO:t0 + TLOC]
        d0 = k * DSL
        maps.append({
            "xT": np.ascontiguousarray(xs),
            "w_inT": w_inT, "w_outT": w_outT,
            "w_gateT": w_gateT, "w_upT": w_upT, "w_downT": w_downT,
            "ffw": ffw, "ffa1": ffa1, "ffa2": ffa2, "ffb": ffb,
            "w_ln1": w_ln1, "w_ln2": w_ln2, "w_mnorm": w_mnorm,
            "w_conv": w_conv, "b_conv": b_conv,
            "dt_bias": dt_bias, "A_log": A_log, "D_rep": D_rep,
            "b_WinT_sl": np.ascontiguousarray(np.concatenate(
                [b_WinT[:, d0:d0 + DSL], b_WinT[:, DI + d0:DI + d0 + DSL]], axis=1)),
            "b_convw_sl": np.ascontiguousarray(b_convw[d0:d0 + DSL]),
            "b_convb_sl": np.ascontiguousarray(b_convb[d0:d0 + DSL]),
            "b_Alog_sl": np.ascontiguousarray(b_Alog[d0:d0 + DSL]),
            "b_WxT_sl": np.ascontiguousarray(b_WxT[d0:d0 + DSL]),
            "b_WoutT": b_WoutT,
        })
    return maps


class Prog:
    def __init__(self, nc, ctx):
        self.nc = nc
        self.ctx = ctx
        self.sems = {}

    def _s(self, name):
        if name not in self.sems:
            h = self.ctx.enter_context(self.nc.semaphore(name))
            self.sems[name] = [h, 0]
        return self.sems[name]

    def inc(self, inst, name, n=1):
        s = self._s(name)
        inst.then_inc(s[0], n)
        s[1] += n
        return inst

    def val(self, name):
        return self._s(name)[1]

    def wait(self, eng, name, v=None):
        s = self._s(name)
        tgt = s[1] if v is None else v
        if tgt > 0:
            eng.wait_ge(s[0], tgt)

    def wait_all(self, eng, names):
        for n in names:
            self.wait(eng, n)


class Banks:

    def __init__(self, pg, banks):
        self.pg = pg
        self.banks = banks
        self.guard = {}
        self.i = 0

    def take(self, eng):
        b = self.banks[self.i % len(self.banks)]
        self.i += 1
        g = self.guard.get(id(b))
        if g is not None:
            self.pg.wait(eng, g[0], v=g[1])
        return b

    def consumed(self, b, sem, val):
        self.guard[id(b)] = (sem, val)


class Arena:

    def __init__(self, tensor, nwords):
        self.t = tensor
        self.n = nwords
        self.off = 0
        self.marks = []

    def alloc(self, words, align=16):
        o = (self.off + align - 1) // align * align
        assert o + words <= self.n, f"arena overflow: {o}+{words} > {self.n}"
        self.off = o + words
        return o

    def mark(self):
        self.marks.append(self.off)

    def release(self):
        self.off = self.marks.pop()

    def f32(self, off, words, parts=128):
        return self.t[0:parts, off:off + words]

    def b16(self, off, words, parts=128):
        return self.t[0:parts, off:off + (words + 1) // 2].bitcast(BF16)[0:parts, 0:words]


DMA_IN = "dI"
DMA_SP = "dS"
DMA_G = "dG"
TM = "T"
VM = "V"
SM = "S"
GM = "G"
CC = "C"
ST = "st"
ALLSEMS = (DMA_IN, DMA_SP, DMA_G, TM, VM, SM, GM, CC)

R_M, R_MSQ, R_VAR, R_RS, R_SIG, R_NMS, R_T1, R_T2 = range(8)
B_SIG, B_NMS, B_T1, B_T2 = range(4)


def build_nc(stages=ALL_STAGES):
    nc = bass.Bass("TRN2", target_bir_lowering=False)
    st = list(stages)

    def din(name, shape):
        return nc.dram_tensor(name, list(shape), F32, kind="ExternalInput").ap()

    d_xT = din("xT", [H, TX])
    d_out = nc.dram_tensor("out_hT", [H, TLOC], F32, kind="ExternalOutput").ap()
    d_w_inT = din("w_inT", [H, DIN])
    d_w_outT = din("w_outT", [E, H])
    d_w_gateT = din("w_gateT", [H, FF])
    d_w_upT = din("w_upT", [H, FF])
    d_w_downT = din("w_downT", [FF, H])
    d_ffw = din("ffw", [H, H])
    d_ffa1 = din("ffa1", [H, H])
    d_ffa2 = din("ffa2", [H, H])
    d_ffb = din("ffb", [H, 1])
    d_wln1 = din("w_ln1", [H, 1])
    d_wln2 = din("w_ln2", [H, 1])
    d_wmn = din("w_mnorm", [E, 1])
    d_wconv = din("w_conv", [CONV, 4])
    d_bconv = din("b_conv", [CONV, 1])
    d_dtb = din("dt_bias", [NH, 1])
    d_Alog = din("A_log", [NH, 1])
    d_Drep = din("D_rep", [E, 1])
    d_bWin = din("b_WinT_sl", [H, 2 * DSL])
    d_bconvw = din("b_convw_sl", [DSL, 4])
    d_bconvb = din("b_convb_sl", [DSL, 1])
    d_bAlog = din("b_Alog_sl", [DSL, DS])
    d_bWx = din("b_WxT_sl", [DSL, 33])
    d_bWout = din("b_WoutT", [DI, H])

    d_acsflat = nc.dram_tensor("acsflat", [6, 4096], F32, kind="Internal").ap()
    dbg = {}
    if "dbg" in stages:
        dbg["y1"] = nc.dram_tensor("dbg_y1", [128, 16 * TLOC], BF16, kind="ExternalOutput").ap()
        dbg["hid"] = nc.dram_tensor("dbg_hid", [128, 8 * TLOC], BF16, kind="ExternalOutput").ap()
        dbg["dts"] = nc.dram_tensor("dbg_dts", [32, TX], F32, kind="ExternalOutput").ap()
        dbg["bf"] = nc.dram_tensor("dbg_bf", [128, TX], BF16, kind="ExternalOutput").ap()
        dbg["cf"] = nc.dram_tensor("dbg_cf", [128, TX], BF16, kind="ExternalOutput").ap()
        dbg["xcs"] = nc.dram_tensor("dbg_xcs", [128, 16 * 640], BF16, kind="ExternalOutput").ap()
        dbg["acs"] = nc.dram_tensor("dbg_acs", [32, 2048], F32, kind="ExternalOutput").ap()
        dbg["s"] = nc.dram_tensor("dbg_s", [128, 2048], F32, kind="ExternalOutput").ap()
        dbg["rows"] = nc.dram_tensor("dbg_rows", [1, 8 * TSUB], F32, kind="ExternalOutput").ap()
        dbg["hin"] = nc.dram_tensor("dbg_hin", [128, 2048], F32, kind="ExternalOutput").ap()
        dbg["gs"] = nc.dram_tensor("dbg_gs", [128, 256], F32, kind="ExternalOutput").ap()
        dbg["suf"] = nc.dram_tensor("dbg_suf", [32, 48], F32, kind="ExternalOutput").ap()
        dbg["xg"] = nc.dram_tensor("dbg_xg", [128, 8 * TX], F32, kind="ExternalOutput").ap()
        dbg["y1m"] = nc.dram_tensor("dbg_y1m", [128, 16 * TSUB], BF16, kind="ExternalOutput").ap()
        dbg["gamc"] = nc.dram_tensor("dbg_gamc", [128, 512], F32, kind="ExternalOutput").ap()
        dbg["hcb"] = nc.dram_tensor("dbg_hcb", [128, 512], BF16, kind="ExternalOutput").ap()
        dbg["ctl2"] = nc.dram_tensor("dbg_ctl2", [128, 1024], BF16, kind="ExternalOutput").ap()
        dbg["acse2"] = nc.dram_tensor("dbg_acse2", [128, 512], F32, kind="ExternalOutput").ap()
        dbg["hin2"] = nc.dram_tensor("dbg_hin2", [128, 2048], F32, kind="ExternalOutput").ap()
        dbg["bu"] = nc.dram_tensor("dbg_bu", [2 * 128, L], BF16, kind="ExternalOutput").ap()
        dbg["bxd"] = nc.dram_tensor("dbg_bxd", [33, L], F32, kind="ExternalOutput").ap()
        dbg["bdel"] = nc.dram_tensor("dbg_bdel", [1, L], F32, kind="ExternalOutput").ap()
        dbg["bgy"] = nc.dram_tensor("dbg_bgy", [2 * 128, L], BF16, kind="ExternalOutput").ap()
        dbg["bcar"] = nc.dram_tensor("dbg_bcar", [128, 32], F32, kind="ExternalOutput").ap()
        dbg["bu2"] = nc.dram_tensor("dbg_bu2", [2 * 128, L], BF16, kind="ExternalOutput").ap()
        dbg["f3"] = nc.dram_tensor("dbg_f3", [H, TLOC], BF16, kind="ExternalOutput").ap()
        dbg["sdab"] = nc.dram_tensor("dbg_sdab", [128, TSUB], F32, kind="ExternalOutput").ap()
        dbg["sib"] = nc.dram_tensor("dbg_sib", [128, TSUB], F32, kind="ExternalOutput").ap()
        dbg["sso"] = nc.dram_tensor("dbg_sso", [128, TSUB], F32, kind="ExternalOutput").ap()
        dbg["shc"] = nc.dram_tensor("dbg_shc", [128, TSUB], BF16, kind="ExternalOutput").ap()
        dbg["sdu"] = nc.dram_tensor("dbg_sdu", [128, TSUB], BF16, kind="ExternalOutput").ap()
        dbg["sbbc"] = nc.dram_tensor("dbg_sbbc", [128, TSUB], BF16, kind="ExternalOutput").ap()
        dbg["scbc"] = nc.dram_tensor("dbg_scbc", [128, TSUB], BF16, kind="ExternalOutput").ap()
        dbg["shb"] = nc.dram_tensor("dbg_shb", [128, TSUB], BF16, kind="ExternalOutput").ap()
        dbg["hcp"] = nc.dram_tensor("dbg_hcp", [128, 16 * TSUB], BF16, kind="ExternalOutput").ap()
        dbg["yv0"] = nc.dram_tensor("dbg_yv0", [128, TSUB], F32, kind="ExternalOutput").ap()
        dbg["szb"] = nc.dram_tensor("dbg_szb", [128, TSUB], BF16, kind="ExternalOutput").ap()
        dbg["szraw"] = nc.dram_tensor("dbg_szraw", [128, TSUB], BF16, kind="ExternalOutput").ap()
        dbg["xbs"] = nc.dram_tensor("dbg_xbs", [128, 8 * TSUB], BF16, kind="ExternalOutput").ap()
    d_wcast = nc.dram_tensor("wcast", [128, 147456], BF16, kind="Internal").ap()
    d_y1sp = nc.dram_tensor("y1spill", [128, 16 * TLOC], BF16, kind="Internal").ap()
    rg = [list(range(W))]
    d_ssd_pub = nc.dram_tensor("ssd_pub", [128, 2064], F32, kind="Internal").ap()
    d_ssd_all = nc.dram_tensor("ssd_all", [W, 128, 2064], F32, kind="Internal",
                               addr_space="Shared").ap()
    d_f3_pub = nc.dram_tensor("f3_pub", [H, TLOC], BF16, kind="Internal").ap()
    d_f3_all = nc.dram_tensor("f3_all", [W, H, TLOC], BF16, kind="Internal",
                              addr_space="Shared").ap()
    d_xdbl_pub = nc.dram_tensor("xdbl_pub", [33, L], F32, kind="Internal").ap()
    d_xdbl_all = nc.dram_tensor("xdbl_all", [33, L], F32, kind="Internal",
                                addr_space="Shared").ap()
    d_gy_pub = nc.dram_tensor("gy_pub", [2 * 128, L], BF16, kind="Internal").ap()
    d_gy_all = nc.dram_tensor("gy_all", [W, 2 * 128, L], BF16, kind="Internal",
                              addr_space="Shared").ap()
    d_zb_sl = nc.dram_tensor("zb_sl", [128, 2 * L], BF16, kind="Internal").ap()
    d_delta = nc.dram_tensor("delta_row", [1, L], F32, kind="Internal").ap()
    d_xdblb = nc.dram_tensor("xdblb", [33, L], BF16, kind="Internal").ap()

    with contextlib.ExitStack() as ctx:
        pg = Prog(nc, ctx)

        def sbuf(name, shape, dt=F32):
            return ctx.enter_context(nc.sbuf_tensor(name, list(shape), dt))

        X = sbuf("X", [128, 8 * TX])
        ROWS = sbuf("ROWS", [1, 8 * TSUB])
        ROWB = sbuf("ROWB", [1, 4 * TSUB], BF16)
        ONESC = sbuf("ONESC", [128, 1], BF16)
        ONESR = sbuf("ONESR", [1, 128], BF16)
        SCL = sbuf("SCL", [128, 96])
        XBS = sbuf("XBS", [128, 8 * TSUB], BF16)
        EV1 = sbuf("EV1", [128, TSUB])
        EV2 = sbuf("EV2", [128, TSUB])
        MRK = sbuf("MRK", [128, 12])
        WROWF = sbuf("WROWF", [1, H], BF16)
        CONVW = sbuf("CONVW", [128, 96])
        CONVB = sbuf("CONVB", [128, 24])
        Z32 = sbuf("Z32", [32, 128])
        AR = Arena(sbuf("ARENA", [128, 27632]), 27632)

        pbk = [ctx.enter_context(nc.psum_tensor(f"pb{i}", [128, TSUB], F32)) for i in range(8)]
        banks = Banks(pg, pbk[0:4])

        block = ctx.enter_context(nc.Block())

        def rows(i):
            return ROWS[0:1, i * TSUB:(i + 1) * TSUB]

        def rowb(i):
            return ROWB[0:1, i * TSUB:(i + 1) * TSUB]

        @block.sync
        def _(sy):
            for f in range(8):
                pg.inc(sy.dma_start(out=X[:, f * TX:(f + 1) * TX],
                                    in_=d_xT[f * 128:(f + 1) * 128, :]), DMA_IN, 16)
            for f in range(8):
                pg.inc(sy.dma_start(out=SCL[:, f:f + 1],
                                    in_=d_ffb[f * 128:(f + 1) * 128, :]), DMA_IN, 16)
            pg.inc(sy.dma_start(out=SCL[0:32, 8:9], in_=d_Alog), DMA_IN, 16)
            pg.inc(sy.dma_start(out=SCL[0:32, 9:10], in_=d_dtb), DMA_IN, 16)
            for f in range(16):
                pg.inc(sy.dma_start(out=SCL[:, 10 + f:11 + f],
                                    in_=d_Drep[f * 128:(f + 1) * 128, :]), DMA_IN, 16)
            for k in range(8):
                pg.inc(sy.dma_start(out=SCL[:, 26 + k:27 + k],
                                    in_=d_wln1[k * 128:(k + 1) * 128, :]), DMA_IN, 16)
            for k in range(16):
                pg.inc(sy.dma_start(out=SCL[:, 34 + k:35 + k],
                                    in_=d_wmn[k * 128:(k + 1) * 128, :]), DMA_IN, 16)
            for k in range(8):
                pg.inc(sy.dma_start(out=SCL[:, 50 + k:51 + k],
                                    in_=d_wln2[k * 128:(k + 1) * 128, :]), DMA_IN, 16)
            for ft in range(16):
                pg.inc(sy.dma_start(out=CONVW[:, ft * 4:(ft + 1) * 4],
                                    in_=d_wconv[ft * 128:(ft + 1) * 128, :]), DMA_IN, 16)
            pg.inc(sy.dma_start(out=CONVW[:, 64:68], in_=d_wconv[2048:2176, :]), DMA_IN, 16)
            pg.inc(sy.dma_start(out=CONVW[:, 68:72], in_=d_wconv[2176:2304, :]), DMA_IN, 16)
            for ft in range(16):
                pg.inc(sy.dma_start(out=CONVB[:, ft:ft + 1],
                                    in_=d_bconv[ft * 128:(ft + 1) * 128, :]), DMA_IN, 16)
            pg.inc(sy.dma_start(out=CONVB[:, 16:17], in_=d_bconv[2048:2176, :]), DMA_IN, 16)
            pg.inc(sy.dma_start(out=CONVB[:, 17:18], in_=d_bconv[2176:2304, :]), DMA_IN, 16)
            for dt_ in range(2):
                pg.inc(sy.dma_start(out=CONVW[:, 72 + dt_ * 4:76 + dt_ * 4],
                                    in_=d_bconvw[dt_ * 128:(dt_ + 1) * 128, :]), DMA_IN, 16)
                pg.inc(sy.dma_start(out=CONVB[:, 18 + dt_:19 + dt_],
                                    in_=d_bconvb[dt_ * 128:(dt_ + 1) * 128, :]), DMA_IN, 16)

        V_XLOAD = 16 * 8

        CONSTS = sbuf("CONSTS", [128, 8])
        _cvals = [1e-5, 1e-6, 0.0, 0.0, 0.0, 0.0, 0.0, 0.0]

        @block.vector
        def _(v):
            v.memset(ONESC[:], 1.0)
            v.memset(Z32[:], 0.0)
            for i, cv in enumerate(_cvals):
                v.memset(CONSTS[:, i:i + 1], float(cv))
                nc.const_aps.register(F32, float(cv), CONSTS[:, i:i + 1])
            pg.inc(v.memset(ONESR[:], 1.0), VM)

        def ln_stats_sub(c0, extra_rms, eps=EPS0, width=TSUB):

            wd = width

            def rw(i):
                return ROWS[0:1, i * TSUB:i * TSUB + wd]

            def rwb(i):
                return ROWB[0:1, i * TSUB:i * TSUB + wd]

            @block.vector
            def _(v):
                pg.wait(v, DMA_IN, v=V_XLOAD)
                pg.wait(v, TM)
                for f in range(8):
                    pg.inc(v.tensor_copy(XBS[:, f * TSUB:f * TSUB + wd],
                                         X[:, f * TX + c0:f * TX + c0 + wd]), VM)

            @block.tensor
            def _(t):
                pg.wait(t, VM)
                for f in range(8):
                    mm = t.matmul(pbk[5][0:1, 0:wd], ONESC[:], XBS[:, f * TSUB:f * TSUB + wd],
                                  start=(f == 0), stop=(f == 7))
                pg.inc(mm, TM)

            @block.scalar
            def _(s):
                pg.wait(s, TM)
                pg.wait(s, VM)
                for f in range(8):
                    pg.inc(s.activation(XBS[:, f * TSUB:f * TSUB + wd],
                                        X[:, f * TX + c0:f * TX + c0 + wd],
                                        AF.Square), SM)

            @block.tensor
            def _(t):
                pg.wait(t, SM)
                for f in range(8):
                    mm = t.matmul(pbk[6][0:1, 0:wd], ONESC[:], XBS[:, f * TSUB:f * TSUB + wd],
                                  start=(f == 0), stop=(f == 7))
                pg.inc(mm, TM)

            @block.vector
            def _(v):
                pg.wait(v, TM)
                v.tensor_scalar(rw(R_M), pbk[5][0:1, 0:wd], 1.0 / H, None, ALU.mult)
                v.tensor_scalar(rw(R_MSQ), pbk[6][0:1, 0:wd], 1.0 / H, None, ALU.mult)
                v.drain()
                v.tensor_tensor(rw(R_T1), rw(R_M), rw(R_M), ALU.mult)
                v.drain()
                pg.inc(v.tensor_tensor(rw(R_VAR), rw(R_MSQ), rw(R_T1), ALU.subtract), VM)

            @block.scalar
            def _(s):
                pg.wait(s, VM)
                s.activation(rw(R_T1), rw(R_VAR), AF.Ln, bias=float(eps))
                pg.inc(s.activation(rw(R_RS), rw(R_T1), AF.Exp, scale=-0.5), SM)

            if extra_rms:
                @block.vector
                def _(v):
                    pg.wait(v, SM)
                    v.tensor_tensor(rw(R_T1), rw(R_RS), rw(R_RS), ALU.mult)
                    v.drain()
                    pg.inc(v.tensor_tensor(rw(R_T2), rw(R_VAR), rw(R_T1), ALU.mult), VM)

                @block.scalar
                def _(s):
                    pg.wait(s, VM)
                    s.activation(rw(R_T2), rw(R_T2), AF.Ln, bias=EPS1)
                    pg.inc(s.activation(rw(R_T2), rw(R_T2), AF.Exp, scale=-0.5), SM)

                @block.vector
                def _(v):
                    pg.wait(v, SM)
                    v.drain()
                    pg.inc(v.tensor_tensor(rw(R_SIG), rw(R_RS), rw(R_T2), ALU.mult), VM)
            else:
                @block.vector
                def _(v):
                    pg.wait(v, SM)
                    pg.inc(v.tensor_copy(rw(R_SIG), rw(R_RS)), VM)

            @block.vector
            def _(v):
                v.drain()
                v.tensor_tensor(rw(R_T1), rw(R_M), rw(R_SIG), ALU.mult)
                v.drain()
                v.tensor_scalar(rw(R_NMS), rw(R_T1), -1.0, None, ALU.mult)
                v.tensor_copy(rwb(B_SIG), rw(R_SIG))
                v.drain()
                pg.inc(v.tensor_copy(rwb(B_NMS), rw(R_NMS)), VM)

            @block.tensor
            def _(t):
                pg.wait(t, VM)
                pg.inc(t.matmul(pbk[7][0:128, 0:wd], ONESR[:], rwb(B_SIG),
                                start=True, stop=True), TM)

            @block.vector
            def _(v):
                pg.wait(v, TM)
                for f in range(8):
                    pg.inc(v.tensor_tensor(XBS[:, f * TSUB:f * TSUB + wd],
                                           X[:, f * TX + c0:f * TX + c0 + wd],
                                           pbk[7][0:128, 0:wd], ALU.mult), VM)

        def load_weight_bf16(dsrc, wdst_b16, nk, mwid, scr_off, gain_cols=None,
                             row_dst=None, row_bank=None, msrc0=0):
            scr = AR.f32(scr_off, 1024)
            for k in range(nk):
                assert mwid <= 1024
                sl = scr[:, 0:mwid]

                @block.sync
                def _(sy, k=k, sl=sl):
                    pg.wait(sy, VM)
                    pg.inc(sy.dma_start(out=sl, in_=dsrc[k * 128:(k + 1) * 128,
                                                         msrc0:msrc0 + mwid]), DMA_IN, 16)

                @block.vector
                def _(v, k=k, sl=sl):
                    pg.wait(v, DMA_IN)
                    if gain_cols is None:
                        pg.inc(v.tensor_copy(wdst_b16[:, k * mwid:(k + 1) * mwid], sl), VM)
                    else:
                        gc = gain_cols[k]
                        pg.inc(v.tensor_scalar(wdst_b16[:, k * mwid:(k + 1) * mwid], sl,
                                               SCL[:, gc:gc + 1], None,
                                               ALU.mult), VM)
            if row_dst is not None:
                for seg0 in range(0, mwid, TSUB):
                    wid = min(TSUB, mwid - seg0)

                    @block.tensor
                    def _(t, seg0=seg0, wid=wid):
                        pg.wait(t, VM)
                        for k in range(nk):
                            mm = t.matmul(pbk[row_bank][0:1, 0:wid], ONESC[:],
                                          wdst_b16[:, k * mwid + seg0:k * mwid + seg0 + wid],
                                          start=(k == 0), stop=(k == nk - 1))
                        pg.inc(mm, TM)

                    @block.vector
                    def _(v, seg0=seg0, wid=wid):
                        pg.wait(v, TM)
                        pg.inc(v.tensor_copy(row_dst[0:1, seg0:seg0 + wid],
                                             pbk[row_bank][0:1, 0:wid]), VM)

        def ffn_stage(tag):
            AR.mark()
            wff_off = AR.alloc(12288)
            h1_off = AR.alloc(2048)
            h2_off = AR.alloc(2048)
            scr_off = AR.alloc(2048)
            WFF = AR.b16(wff_off, 24 * H)
            H1B = AR.b16(h1_off, 8 * TSUB)
            H2B = AR.b16(h2_off, 8 * TSUB)

            for wi, dw in enumerate([d_ffw, d_ffa1, d_ffa2]):
                load_weight_bf16(dw, AR.b16(wff_off + wi * 4096, 8 * H), 8, H, scr_off,
                                 row_dst=(WROWF if wi == 0 else None), row_bank=4)

            def mm_level(src_sb, wbase, dst_cb, rank1=False):
                for m in range(8):
                    @block.tensor
                    def _(t, m=m):
                        pg.wait(t, VM)
                        b = banks.take(t)
                        for k in range(8):
                            mm = t.matmul(b[0:128, :],
                                          WFF[:, wbase + k * H + m * 128:wbase + k * H + m * 128 + 128],
                                          src_sb[:, k * TSUB:(k + 1) * TSUB],
                                          start=(k == 0), stop=(k == 7) and not rank1)
                        if rank1:
                            mm = t.matmul(b[0:128, :], WROWF[0:1, m * 128:(m + 1) * 128],
                                          rowb(B_NMS), start=False, stop=True)
                        pg.inc(mm, TM)
                    dst_cb(m, banks.banks[(banks.i - 1) % len(banks.banks)])

            for sub in range(NSUB):
                c0 = HALO + sub * TSUB
                ln_stats_sub(c0, extra_rms=False, eps=EPS0)

                def evac_h1(m, b):
                    @block.vector
                    def _(v, m=m, b=b):
                        pg.wait(v, TM)
                        pg.inc(v.tensor_copy(H1B[:, m * TSUB:(m + 1) * TSUB], b[0:128, :]), VM)
                        banks.consumed(b, VM, pg.val(VM))

                mm_level(XBS, 0, evac_h1, rank1=True)

                def evac_h2(m, b):
                    @block.vector
                    def _(v, m=m, b=b):
                        pg.wait(v, TM)
                        pg.inc(v.tensor_copy(H2B[:, m * TSUB:(m + 1) * TSUB], b[0:128, :]), VM)
                        banks.consumed(b, VM, pg.val(VM))

                mm_level(H1B, 8 * H, evac_h2)

                def evac_h3(m, b, sub=sub):
                    ev = EV1 if m % 2 == 0 else EV2

                    @block.scalar
                    def _(s, m=m, b=b, ev=ev):
                        pg.wait(s, TM)
                        pg.wait(s, VM)
                        pg.wait(s, DMA_IN, v=16 * 16)
                        pg.inc(s.activation(ev[:], b[0:128, :], AF.Tanh,
                                            bias=SCL[:, m:m + 1]), SM)
                        banks.consumed(b, SM, pg.val(SM))

                    @block.vector
                    def _(v, m=m, ev=ev, sub=sub):
                        pg.wait(v, SM)
                        cc = HALO + sub * TSUB
                        pg.inc(v.tensor_tensor(X[:, m * TX + cc:m * TX + cc + TSUB],
                                               X[:, m * TX + cc:m * TX + cc + TSUB],
                                               ev[:], ALU.add), VM)

                mm_level(H2B, 16 * H, evac_h3)
            AR.release()


        PLIST = [(0, 512), (512, 512), (1024, 512), (1536, 512), (2048, 3)]
        CHUNK_SCHED = [[0, 1, 2], [3, 4, 5, 6], [7, 8, 9, 10], [11, 12, 13, 14], [15]]
        RSUBS = [HALO + s * TSUB for s in range(4)]

        def emit_attn_a(xcore=True):
            AR.mark()
            cf_off = AR.alloc(1026)
            blk_off = AR.alloc(2600)
            AR.mark()
            bf_off = AR.alloc(1026)
            S_off = AR.alloc(2048)
            hsv_off = AR.alloc(48)
            bscr_off = AR.alloc(16)
            xcs_off = AR.alloc(5120)
            xt_off = AR.alloc(1024)
            xtw_off = AR.alloc(256)
            acse_off = AR.alloc(512)
            crep_off = AR.alloc(256)
            ctl_off = AR.alloc(512)
            y1c_off = AR.alloc(512)
            dif_off = AR.alloc(512)
            g8_off = AR.alloc(512)
            sb16_off = AR.alloc(256)
            gbc_off = AR.alloc(32)
            tstk_off = AR.alloc(128)
            tall_off = AR.alloc(128)
            tallb_off = AR.alloc(16)
            bctb_off = AR.alloc(64)
            btb_off = AR.alloc(64)
            maskc_off = AR.alloc(128)
            idb_off = AR.alloc(64)
            id32_off = AR.alloc(128)
            wrx_off = AR.alloc(1024)
            wrbc_off = AR.alloc(160)

            BF = AR.b16(bf_off, 2 * 1026)[0:128, 0:TX]
            CF = AR.b16(cf_off, 2 * 1026)[0:128, 0:TX]
            DTS = AR.t[0:32, blk_off:blk_off + TX]
            ACS = AR.t[32:64, blk_off:blk_off + 2048]
            ATOT = AR.t[32:64, blk_off + 2048:blk_off + 2049]
            SCR_A = AR.t[64:96, blk_off:blk_off + 2052]
            SCR_B = AR.t[96:128, blk_off:blk_off + 2052]
            DA0 = AR.t[0:32, blk_off + 2052:blk_off + 2180]
            E1S = AR.t[0:32, blk_off + 2180:blk_off + 2308]
            W0S = AR.t[0:32, blk_off + 2308:blk_off + 2436]
            P0S = AR.t[0:32, blk_off + 2436:blk_off + 2564]
            S = AR.f32(S_off, 2048)
            HSV = AR.b16(hsv_off, 96)
            BSCR = AR.b16(bscr_off, 32)
            XCS = AR.b16(xcs_off, 2 * 5120)
            XT = AR.b16(xt_off, 2048)
            XTW = AR.b16(xtw_off, 512)
            ACSE = AR.f32(acse_off, 512)
            CREP = AR.b16(crep_off, 512)
            CTL = AR.b16(ctl_off, 1024)
            Y1C = AR.b16(y1c_off, 1024)
            DIF = AR.f32(dif_off, 512)
            LMQ = AR.b16(dif_off, 512)
            G8 = AR.b16(g8_off, 1024)
            SB16G = AR.b16(sb16_off, 512)
            GBC = AR.f32(gbc_off, 32)
            TSTK = AR.f32(tstk_off, 128)
            TALL = AR.f32(tall_off, 128)
            TALLB = AR.b16(tallb_off, 32)
            BCTB = AR.b16(bctb_off, 128)
            BTB = AR.b16(btb_off, 128)
            MASKC = AR.f32(maskc_off, 128)
            IDENTB = AR.b16(idb_off, 128)
            IDENT32 = AR.f32(id32_off, 128)
            WRX = AR.b16(wrx_off, 2048, parts=1)
            WRBC = AR.b16(wrbc_off, 320, parts=1)

            def xcs(ft, lc, wid):
                return XCS[:, ft * 640 + lc:ft * 640 + lc + wid]

            @block.gpsimd
            def _(g):
                pg.wait(g, DMA_IN, v=V_XLOAD)
                pg.inc(g.iota(DIF[:, 0:128], [[-1, 128]], base=0, channel_multiplier=1,
                              allow_small_or_imprecise_dtypes=True), GM)

            @block.vector
            def _(v):
                pg.wait(v, GM)
                v.tensor_scalar(IDENT32[:], DIF[:, 0:128], 0.0, None, ALU.is_equal)
                v.tensor_copy(IDENTB[:], IDENT32[:])
                v.tensor_scalar(MASKC[:], DIF[:, 0:128], 0.0, None, ALU.is_le)
                v.tensor_scalar(MASKC[:], MASKC[:], 1.0, 3.0e38, ALU.subtract, ALU.mult)
                v.memset(S[:], 0.0)
                pg.inc(v.memset(ATOT[:], 0.0), VM)

            @block.scalar
            def _(s):
                pg.wait(s, DMA_IN)
                pg.inc(s.activation(SCL[0:32, 8:9], SCL[0:32, 8:9], AF.Exp), SM)

            @block.vector
            def _(v):
                pg.wait(v, SM)
                pg.inc(v.tensor_scalar(SCL[0:32, 8:9], SCL[0:32, 8:9], -1.0, None,
                                       ALU.mult), VM)

            AR.mark()
            wgbc_off = AR.alloc(1152)
            scr_off = AR.alloc(1024)
            WGBC = AR.b16(wgbc_off, 8 * 288)
            load_weight_bf16(d_w_inT, WGBC, 8, 288, scr_off,
                             gain_cols=[26 + k for k in range(8)],
                             row_dst=WRBC, row_bank=4, msrc0=4096)
            for (c0, wid) in PLIST:
                ln_stats_sub(c0, extra_rms=True, eps=EPS0, width=wid)
                for mi in range(3):
                    mw = 32 if mi == 2 else 128

                    @block.tensor
                    def _(t, mi=mi, mw=mw, wid=wid):
                        pg.wait(t, VM)
                        b = banks.take(t)
                        for k in range(8):
                            t.matmul(b[0:mw, 0:wid],
                                     WGBC[:, k * 288 + mi * 128:k * 288 + mi * 128 + mw],
                                     XBS[:, k * TSUB:k * TSUB + wid],
                                     start=(k == 0), stop=False)
                        mm = t.matmul(b[0:mw, 0:wid],
                                      WRBC[0:1, mi * 128:mi * 128 + mw],
                                      ROWB[0:1, B_NMS * TSUB:B_NMS * TSUB + wid],
                                      start=False, stop=True)
                        pg.inc(mm, TM)

                    bnk = banks.banks[(banks.i - 1) % len(banks.banks)]

                    @block.vector
                    def _(v, mi=mi, mw=mw, c0=c0, wid=wid, b=bnk):
                        pg.wait(v, TM)
                        out = [BF[:, c0:c0 + wid], CF[:, c0:c0 + wid],
                               DTS[:, c0:c0 + wid]][mi]
                        pg.inc(v.tensor_copy(out[0:mw, 0:wid] if mi == 2 else out,
                                             b[0:mw, 0:wid]), VM)
                        banks.consumed(b, VM, pg.val(VM))

            @block.scalar
            def _(s):
                pg.wait(s, VM)
                s.activation(SCR_A[:, 0:TX], DTS[:, 0:TX], AF.Abs, bias=SCL[0:32, 9:10])
                s.activation(SCR_A[:, 0:TX], SCR_A[:, 0:TX], AF.Exp, scale=-1.0)
                s.activation(SCR_A[:, 0:TX], SCR_A[:, 0:TX], AF.Ln, bias=1.0)
                pg.inc(s.activation(DTS[:, 0:TX], DTS[:, 0:TX], AF.Relu,
                                    bias=SCL[0:32, 9:10]), SM)

            for pc0 in range(0, TX, 128):
                pw = min(128, TX - pc0)

                @block.vector
                def _(v, pc0=pc0, pw=pw):
                    pg.wait(v, SM)
                    v.tensor_copy(P0S[:, 0:pw], SCR_A[:, pc0:pc0 + pw])
                    v.drain()
                    pg.inc(v.tensor_tensor(DTS[:, pc0:pc0 + pw], DTS[:, pc0:pc0 + pw],
                                           P0S[:, 0:pw], ALU.add), VM)

            for (c0, wid) in reversed(PLIST):
                g0 = max(c0, 3)
                wcv = c0 + wid - g0
                if wcv <= 0:
                    continue
                for ti, buf in enumerate([BF, CF]):
                    @block.vector
                    def _(v, ti=ti, buf=buf, g0=g0, wcv=wcv):
                        pg.wait(v, VM)
                        pg.wait(v, SM)
                        v.tensor_scalar(EV1[:, 0:wcv], buf[:, g0 - 3:g0 - 3 + wcv],
                                        CONVW[:, 64 + ti * 4:65 + ti * 4], None, ALU.mult)
                        for kk in range(1, 4):
                            v.scalar_tensor_tensor(EV1[:, 0:wcv],
                                                   buf[:, g0 - 3 + kk:g0 - 3 + kk + wcv],
                                                   CONVW[:, 64 + ti * 4 + kk:65 + ti * 4 + kk],
                                                   EV1[:, 0:wcv], ALU.mult, ALU.add)
                        pg.inc(v.memset(MRK[:, 0:1], 0.0), VM)

                    @block.scalar
                    def _(s, ti=ti, buf=buf, g0=g0, wcv=wcv):
                        pg.wait(s, VM)
                        pg.inc(s.activation(buf[:, g0:g0 + wcv], EV1[:, 0:wcv], AF.Silu,
                                            bias=CONVB[:, 16 + ti:17 + ti]), SM)

            AR.release()

            AR.mark()
            wgx_off = AR.alloc(8192)
            scr_off = AR.alloc(1024)
            WGXa = AR.b16(wgx_off, 8 * 1024)
            WGXb = AR.b16(wgx_off + 4096, 8 * 1024)
            load_weight_bf16(d_w_inT, WGXa, 8, 1024, scr_off,
                             gain_cols=[26 + k for k in range(8)],
                             row_dst=AR.b16(wrx_off, 1024, parts=1), row_bank=4,
                             msrc0=2048)
            load_weight_bf16(d_w_inT, WGXb, 8, 1024, scr_off,
                             gain_cols=[26 + k for k in range(8)],
                             row_dst=AR.b16(wrx_off + 512, 1024, parts=1), row_bank=4,
                             msrc0=3072)

            acse_guard = [0, 0]
            crep_guard = [0]

            def ssd_chunk(c, sub_i, sub_c0):
                glob = 3 + c * Q
                base = glob - sub_c0 + (128 if sub_i > 0 else 0)

                @block.vector
                def _(v):
                    pg.wait(v, VM)
                    pg.wait(v, SM)
                    v.tensor_scalar(DA0[:], DTS[:, glob:glob + 128],
                                    SCL[0:32, 8:9], None, ALU.mult)
                    v.drain()
                    v.tensor_tensor_scan(ACS[:, c * 128:(c + 1) * 128], DA0[:],
                                         Z32[:], 0.0, ALU.add, ALU.add)
                    v.drain()
                    pg.inc(v.memset(MRK[:, 9:10], 0.0), VM)

                @block.scalar
                def _(s):
                    pg.wait(s, VM)
                    pg.inc(s.activation(E1S[:], ACS[:, c * 128:(c + 1) * 128],
                                        AF.Exp, scale=-1.0,
                                        bias=ACS[:, c * 128 + 127:c * 128 + 128]), SM)

                @block.vector
                def _(v):
                    pg.wait(v, SM)
                    v.tensor_tensor(W0S[:], E1S[:], DTS[:, glob:glob + 128], ALU.mult)
                    v.tensor_copy(TSTK[0:32, :], ACS[:, c * 128:(c + 1) * 128])
                    v.drain()
                    v.tensor_copy(TSTK[32:64, :], W0S[:])
                    pg.inc(v.tensor_copy(TSTK[64:96, :], DTS[:, glob:glob + 128]), VM)

                @block.tensor
                def _(t):
                    pg.wait(t, VM)
                    b = banks.take(t)
                    t.transpose(b[0:128, 0:128], TSTK[:], IDENT32[:])
                    pg.inc(t.transpose(b[:].bitcast(BF16)[0:128, 256:384],
                                       BF[:, glob:glob + 128], IDENTB[:]), TM)

                bnkT = banks.banks[(banks.i - 1) % len(banks.banks)]

                @block.vector
                def _(v, b=bnkT):
                    pg.wait(v, TM)
                    v.tensor_copy(TALL[:], b[0:128, 0:128])
                    v.tensor_copy(TALLB[:], TALL[:, 64:96])
                    pg.inc(v.tensor_copy(BTB[:], b[:].bitcast(BF16)[0:128, 256:384]), VM)
                    banks.consumed(b, VM, pg.val(VM))

                @block.tensor
                def _(t):
                    pg.wait(t, VM)
                    b = banks.take(t)
                    pg.inc(t.matmul(b[0:128, 0:128], BF[:, glob:glob + 128],
                                    CF[:, glob:glob + 128], start=True, stop=True), TM)

                bnkB = banks.banks[(banks.i - 1) % len(banks.banks)]

                @block.vector
                def _(v, b=bnkB):
                    pg.wait(v, TM)
                    pg.inc(v.tensor_copy(BCTB[:], b[0:128, 0:128]), VM)
                    banks.consumed(b, VM, pg.val(VM))

                for tg in range(4):
                    @block.tensor
                    def _(t, tg=tg):
                        pg.wait(t, VM)
                        pg.wait(t, SM)
                        b = banks.take(t)
                        for j in range(4):
                            mm = t.transpose(b[:].bitcast(BF16)[0:128, j * 128:(j + 1) * 128],
                                             xcs(tg * 4 + j, base, 128), IDENTB[:])
                        pg.inc(mm, TM)

                    bnkX = banks.banks[(banks.i - 1) % len(banks.banks)]

                    @block.scalar
                    def _(s, tg=tg, b=bnkX):
                        pg.wait(s, TM)
                        pg.inc(s.activation(XT[:, tg * 512:(tg + 1) * 512],
                                            b[:].bitcast(BF16)[0:128, 0:512], AF.Copy), SM)
                        banks.consumed(b, SM, pg.val(SM))

                @block.sync
                def _(sy):
                    pg.wait(sy, VM)
                    pg.wait(sy, SM)
                    pg.inc(sy.dma_start(out=d_acsflat[c % 4:c % 4 + 1, 0:4096],
                                        in_=ACS[:, c * 128:(c + 1) * 128]), DMA_SP, 16)
                    pg.wait(sy, VM, v=crep_guard[0])
                    pg.inc(sy.dma_start(
                        out=CREP[:],
                        in_=CF[:, glob:glob + 128].unsqueeze(1).broadcast_to([128, 4, 128])),
                        DMA_SP, 16)

                for qd in range(8):
                    h0 = qd * 4

                    @block.sync
                    def _(sy, qd=qd, h0=h0):
                        pg.wait(sy, DMA_SP)
                        pg.wait(sy, SM, v=acse_guard[0])
                        pg.wait(sy, VM, v=acse_guard[1])
                        pg.inc(sy.dma_start(
                            out=ACSE[:],
                            in_=d_acsflat[c % 4, h0 * 128:(h0 + 4) * 128]
                            .unsqueeze(0).broadcast_to([128, 512])), DMA_SP, 16)

                    @block.vector
                    def _(v, h0=h0):
                        pg.wait(v, DMA_SP)
                        pg.wait(v, SM)
                        pg.wait(v, TM)
                        for j in range(4):
                            v.scalar_tensor_tensor(DIF[:, j * 128:(j + 1) * 128],
                                                   ACSE[:, j * 128:(j + 1) * 128],
                                                   TALL[:, h0 + j:h0 + j + 1],
                                                   MASKC[:], ALU.subtract, ALU.min)
                        pg.inc(v.memset(MRK[:, 1:2], 0.0), VM)

                    @block.scalar
                    def _(s):
                        pg.wait(s, VM)
                        s.activation(LMQ[:], DIF[:], AF.Exp)
                        pg.inc(s.activation(ACSE[:], ACSE[:], AF.Exp), SM)
                    acse_guard[0] = pg.val(SM)

                    @block.vector
                    def _(v, qd=qd, h0=h0):
                        pg.wait(v, SM)
                        v.tensor_tensor(CTL[:, (qd % 2) * 512:(qd % 2) * 512 + 512],
                                        CREP[:], ACSE[:], ALU.mult)
                        for j in range(4):
                            v.scalar_tensor_tensor(
                                G8[:, (qd % 2) * 512 + j * 128:(qd % 2) * 512 + (j + 1) * 128],
                                LMQ[:, j * 128:(j + 1) * 128],
                                TALLB[:, h0 + j:h0 + j + 1],
                                BCTB[:], ALU.mult, ALU.mult)
                        pg.inc(v.tensor_copy(
                            GBC[:, h0:h0 + 4],
                            ACSE[:].rearrange("p (a b) -> p a b", b=128)[:, :, 127:128]), VM)
                    acse_guard[1] = pg.val(VM)
                    crep_guard[0] = pg.val(VM)

                    if qd % 2 == 1:
                        g = qd // 2

                        @block.vector
                        def _(v, g=g):
                            pg.wait(v, VM)
                            v.tensor_copy(SB16G[:], S[:, g * 512:(g + 1) * 512])
                            for j in range(8):
                                h = g * 8 + j
                                v.tensor_scalar(XTW[:, j * 64:(j + 1) * 64],
                                                XT[:, h * 64:(h + 1) * 64],
                                                TALL[:, 32 + h:33 + h], None, ALU.mult)
                            pg.inc(v.memset(MRK[:, 2:3], 0.0), VM)

                        @block.tensor
                        def _(t, g=g):
                            pg.wait(t, VM)
                            b = banks.take(t)
                            for j in range(8):
                                h = g * 8 + j
                                pr = (j % 2) * 64
                                sg = (j // 2) * 128
                                tp = (0, 64) if pr else None
                                t.matmul(b[pr:pr + 64, sg:sg + 128],
                                         XT[:, h * 64:(h + 1) * 64],
                                         G8[:, (j // 4) * 512 + (j % 4) * 128:
                                            (j // 4) * 512 + (j % 4) * 128 + 128],
                                         start=True, stop=False, tile_position=tp)
                                mm = t.matmul(b[pr:pr + 64, sg:sg + 128],
                                              SB16G[:, j * 64:(j + 1) * 64],
                                              CTL[:, (j // 4) * 512 + (j % 4) * 128:
                                                  (j // 4) * 512 + (j % 4) * 128 + 128],
                                              start=False, stop=True, tile_position=tp)
                            pg.inc(mm, TM)

                        bnkY = banks.banks[(banks.i - 1) % len(banks.banks)]

                        @block.vector
                        def _(v, g=g, b=bnkY):
                            pg.wait(v, TM)
                            pg.wait(v, DMA_SP)
                            for sg in range(4):
                                ft = g * 4 + sg
                                v.scalar_tensor_tensor(
                                    Y1C[:, (g % 2) * 512 + sg * 128:(g % 2) * 512 + (sg + 1) * 128],
                                    xcs(ft, base, 128), SCL[:, 10 + ft:11 + ft],
                                    b[0:128, sg * 128:(sg + 1) * 128], ALU.mult, ALU.add)
                            pg.inc(v.memset(MRK[:, 3:4], 0.0), VM)
                            banks.consumed(b, VM, pg.val(VM))

                        @block.tensor
                        def _(t, g=g):
                            pg.wait(t, VM)
                            b = banks.take(t)
                            for j in range(8):
                                mm = t.matmul(b[0:128, j * 64:(j + 1) * 64], BTB[:],
                                              XTW[:, j * 64:(j + 1) * 64],
                                              start=True, stop=True)
                            pg.inc(mm, TM)

                        bnkS = banks.banks[(banks.i - 1) % len(banks.banks)]

                        @block.vector
                        def _(v, g=g, b=bnkS):
                            pg.wait(v, TM)
                            for j in range(8):
                                h = g * 8 + j
                                v.scalar_tensor_tensor(S[:, h * 64:(h + 1) * 64],
                                                       S[:, h * 64:(h + 1) * 64],
                                                       GBC[:, h:h + 1],
                                                       b[0:128, j * 64:(j + 1) * 64],
                                                       ALU.mult, ALU.add)
                            pg.inc(v.memset(MRK[:, 4:5], 0.0), VM)
                            banks.consumed(b, VM, pg.val(VM))

                        if g % 2 == 1:
                            @block.scalar
                            def _(s, g=g):
                                pg.wait(s, VM)
                                pg.inc(s.dma_start(
                                    out=d_y1sp.rearrange("p (f t) -> p f t", t=TLOC)
                                    [:, (g - 1) * 4:(g + 1) * 4, c * 128:(c + 1) * 128],
                                    in_=Y1C[:].rearrange("p (f t) -> p f t", t=128)),
                                    DMA_SP, 16)

            for sub_i, (c0, wid) in enumerate(PLIST):
                off = 128 if sub_i > 0 else 0
                hs_cur = (sub_i % 2) * 48
                hs_prev = ((sub_i + 1) % 2) * 48
                ln_stats_sub(c0, extra_rms=True, eps=EPS0, width=wid)
                if sub_i > 0:
                    shift_src = 384 if sub_i == 1 else 512

                    @block.vector
                    def _(v, shift_src=shift_src):
                        pg.wait(v, VM)
                        pg.wait(v, TM)
                        pg.wait(v, SM)
                        for ft in range(16):
                            v.tensor_copy(xcs(ft, 0, 128), xcs(ft, shift_src, 128))
                        pg.inc(v.memset(MRK[:, 5:6], 0.0), VM)

                for m in range(16):
                    wa = WGXa if m < 8 else WGXb
                    mm0 = (m % 8) * 128

                    @block.tensor
                    def _(t, m=m, wa=wa, mm0=mm0, wid=wid):
                        pg.wait(t, VM)
                        b = banks.take(t)
                        for k in range(8):
                            t.matmul(b[0:128, 0:wid],
                                     wa[:, k * 1024 + mm0:k * 1024 + mm0 + 128],
                                     XBS[:, k * TSUB:k * TSUB + wid],
                                     start=(k == 0), stop=False)
                        mm = t.matmul(b[0:128, 0:wid],
                                      AR.b16(wrx_off, 2048, parts=1)[0:1, m * 128:(m + 1) * 128],
                                      ROWB[0:1, B_NMS * TSUB:B_NMS * TSUB + wid],
                                      start=False, stop=True)
                        pg.inc(mm, TM)

                    bnkP = banks.banks[(banks.i - 1) % len(banks.banks)]

                    @block.scalar
                    def _(s, m=m, wid=wid, off=off, b=bnkP):
                        pg.wait(s, TM)
                        pg.inc(s.activation(xcs(m, off, wid), b[0:128, 0:wid], AF.Copy), SM)
                        banks.consumed(b, SM, pg.val(SM))

                for ft in range(16):
                    @block.vector
                    def _(v, ft=ft, sub_i=sub_i, wid=wid, off=off,
                          hs_cur=hs_cur, hs_prev=hs_prev):
                        pg.wait(v, SM)
                        if sub_i == 0:
                            wcv = wid - 3
                            v.tensor_scalar(EV1[:, 3:3 + wcv], xcs(ft, 0, wcv),
                                            CONVW[:, ft * 4:ft * 4 + 1], None, ALU.mult)
                            for kk in range(1, 4):
                                v.scalar_tensor_tensor(EV1[:, 3:3 + wcv],
                                                       xcs(ft, kk, wcv),
                                                       CONVW[:, ft * 4 + kk:ft * 4 + kk + 1],
                                                       EV1[:, 3:3 + wcv], ALU.mult, ALU.add)
                        else:
                            v.tensor_copy(BSCR[:, 0:3], HSV[:, hs_prev + ft * 3:
                                                            hs_prev + ft * 3 + 3])
                            v.tensor_copy(BSCR[:, 3:9], xcs(ft, 128, 6))
                            v.tensor_scalar(EV1[:, 0:3], BSCR[:, 0:3],
                                            CONVW[:, ft * 4:ft * 4 + 1], None, ALU.mult)
                            for kk in range(1, 4):
                                v.scalar_tensor_tensor(EV1[:, 0:3], BSCR[:, kk:kk + 3],
                                                       CONVW[:, ft * 4 + kk:ft * 4 + kk + 1],
                                                       EV1[:, 0:3], ALU.mult, ALU.add)
                            if wid > 3:
                                wcv = wid - 3
                                v.tensor_scalar(EV1[:, 3:3 + wcv], xcs(ft, 128, wcv),
                                                CONVW[:, ft * 4:ft * 4 + 1], None, ALU.mult)
                                for kk in range(1, 4):
                                    v.scalar_tensor_tensor(
                                        EV1[:, 3:3 + wcv], xcs(ft, 128 + kk, wcv),
                                        CONVW[:, ft * 4 + kk:ft * 4 + kk + 1],
                                        EV1[:, 3:3 + wcv], ALU.mult, ALU.add)
                        pg.inc(v.tensor_copy(HSV[:, hs_cur + ft * 3:hs_cur + ft * 3 + 3],
                                             xcs(ft, off + wid - 3, 3)), VM)

                    @block.scalar
                    def _(s, ft=ft, sub_i=sub_i, wid=wid, off=off):
                        pg.wait(s, VM)
                        if sub_i == 0:
                            pg.inc(s.activation(xcs(ft, 3, wid - 3), EV1[:, 3:wid],
                                                AF.Silu, bias=CONVB[:, ft:ft + 1]), SM)
                        else:
                            pg.inc(s.activation(xcs(ft, 128, wid), EV1[:, 0:wid],
                                                AF.Silu, bias=CONVB[:, ft:ft + 1]), SM)

                for c in CHUNK_SCHED[sub_i]:
                    ssd_chunk(c, sub_i, c0)

            if dbg:
                @block.gpsimd
                def _(g):
                    pg.wait_all(g, (VM, SM, TM, DMA_SP))
                    pg.inc(g.dma_start(out=dbg["y1"], in_=d_y1sp[:, :]), DMA_G, 16)
                    pg.inc(g.dma_start(out=dbg["dts"], in_=DTS[:, 0:TX]), DMA_G, 16)
                    pg.inc(g.dma_start(out=dbg["bf"], in_=BF[:, 0:TX]), DMA_G, 16)
                    pg.inc(g.dma_start(out=dbg["cf"], in_=CF[:, 0:TX]), DMA_G, 16)
                    pg.inc(g.dma_start(out=dbg["xcs"], in_=XCS[0:128, 0:16 * 640]), DMA_G, 16)
                    pg.inc(g.dma_start(out=dbg["acs"], in_=ACS[:, 0:2048]), DMA_G, 16)
                    pg.inc(g.dma_start(out=dbg["s"], in_=S[:]), DMA_G, 16)
                    pg.inc(g.dma_start(out=dbg["rows"], in_=ROWS[:]), DMA_G, 16)
                    pg.inc(g.dma_start(out=dbg["xbs"], in_=XBS[:]), DMA_G, 16)
                    pg.wait(g, DMA_G)
            AR.release()
            if xcore:
                @block.vector
                def _(v):
                    pg.wait(v, SM)
                    pg.wait(v, TM)
                    v.tensor_copy(P0S[:, 0:16],
                                  ACS[:].rearrange("p (a b) -> p a b", b=128)[:, :, 127:128])
                    v.drain()
                    pg.inc(v.tensor_reduce(ATOT[:], P0S[:, 0:16], AX.X, ALU.add), VM)

                @block.gpsimd
                def _(g):
                    pg.wait_all(g, (VM, SM, TM, DMA_SP))
                    pg.inc(g.dma_start(out=d_ssd_pub[:, 0:2048], in_=S[:]), DMA_G, 16)
                    pg.inc(g.dma_start(out=d_ssd_pub[0:32, 2048:2049],
                                       in_=ATOT[:], allow_slow_non_contiguous=True),
                           DMA_G, 16)
                    pg.wait(g, DMA_G)
                    pg.inc(g.collective_compute("AllGather", ALU.bypass,
                                                replica_groups=rg,
                                                ins=[d_ssd_pub], outs=[d_ssd_all]), CC)
            AR.release()
            emit_attn_a_tail(xcore, blk_off, cf_off)
            AR.release()

        def cast_weights_to_dram(dsrc, dstcol0, nk, mwid, scr_off, gain_cols=None,
                                 msrc0=0, row_acc_off=None):
            scr = AR.f32(scr_off, 1024)
            scrb = AR.b16(scr_off + 1024, 2048)
            for k in range(nk):
                @block.sync
                def _(sy, k=k):
                    pg.wait(sy, DMA_SP)
                    pg.inc(sy.dma_start(out=scr[:, 0:mwid],
                                        in_=dsrc[k * 128:(k + 1) * 128,
                                                 msrc0:msrc0 + mwid]), DMA_IN, 16)

                @block.vector
                def _(v, k=k):
                    pg.wait(v, DMA_IN)
                    if gain_cols is None:
                        pg.inc(v.tensor_copy(scrb[0:128, 0:mwid], scr[:, 0:mwid]), VM)
                    else:
                        gc = gain_cols[k]
                        pg.inc(v.tensor_scalar(scrb[0:128, 0:mwid], scr[:, 0:mwid],
                                               SCL[:, gc:gc + 1], None, ALU.mult), VM)

                if row_acc_off is not None:
                    for sg in range(0, mwid, TSUB):
                        wdd = min(TSUB, mwid - sg)

                        @block.tensor
                        def _(t, k=k, sg=sg, wdd=wdd):
                            pg.wait(t, VM)
                            pg.inc(t.matmul(pbk[4][0:1, 0:wdd], ONESC[:],
                                            scrb[0:128, sg:sg + wdd],
                                            start=True, stop=True), TM)

                        @block.vector
                        def _(v, k=k, sg=sg, wdd=wdd):
                            pg.wait(v, TM)
                            racc = AR.f32(row_acc_off, mwid, parts=1)
                            if k == 0:
                                pg.inc(v.tensor_copy(racc[0:1, sg:sg + wdd],
                                                     pbk[4][0:1, 0:wdd]), VM)
                            else:
                                pg.inc(v.tensor_tensor(racc[0:1, sg:sg + wdd],
                                                       racc[0:1, sg:sg + wdd],
                                                       pbk[4][0:1, 0:wdd], ALU.add), VM)

                @block.scalar
                def _(s, k=k):
                    pg.wait(s, VM)
                    pg.inc(s.dma_start(out=d_wcast[:, dstcol0 + k * mwid:
                                                   dstcol0 + (k + 1) * mwid],
                                       in_=scrb[0:128, 0:mwid]), DMA_SP, 16)

        WC_Z, WC_OUT, WC_GATE, WC_UP, WC_DOWN = 0, 16384, 32768, 65536, 98304

        def emit_attn_a_tail(xcore, blk_off, cf_off):
            CF = AR.b16(cf_off, 2 * 1026)[0:128, 0:TX]
            ACS = AR.t[32:64, blk_off:blk_off + 2048]
            SCR_A = AR.t[64:96, blk_off:blk_off + 2052]
            SCR_B = AR.t[96:128, blk_off:blk_off + 2052]

            AR.mark()
            hfull_off = AR.alloc(8192)
            HFULL = AR.b16(hfull_off, 8 * TLOC)

            AR.mark()
            y1m_off = AR.alloc(4096)
            wz_off = AR.alloc(1024)
            wo_off = AR.alloc(1024)
            evz_off = AR.alloc(256)
            sq_off = AR.alloc(256)
            rs0_off = AR.alloc(512)
            mrs0_off = AR.alloc(512)
            rsm_off = AR.alloc(512)
            hin_off = AR.alloc(2048)
            hcb_off = AR.alloc(256)
            sj_off = AR.alloc(1024)
            gs_off = AR.alloc(256)
            gamc_off = AR.alloc(512)
            acse2_off = AR.alloc(512)
            crep2_off = AR.alloc(256)
            ctl2_off = AR.alloc(512)

            Y1M = AR.b16(y1m_off, 16 * TSUB)
            EVZ = AR.b16(evz_off, TSUB)
            RS0BC = AR.f32(rs0_off, TSUB)
            MRS0BC = AR.f32(mrs0_off, TSUB)
            RSMBC = AR.f32(rsm_off, TSUB)
            HIN = AR.f32(hin_off, 2048)
            HCB = AR.b16(hcb_off, 512)
            SJ = AR.f32(sj_off, 1024)
            GS = AR.f32(gs_off, 256)
            GAMC = AR.f32(gamc_off, 512)
            ACSE2 = AR.f32(acse2_off, 512)
            CREP2 = AR.b16(crep2_off, 512)
            CTL2 = AR.b16(ctl2_off, 1024)
            acse2_guard = [0, 0]
            crep2_guard = [0]

            AR.mark()
            scrw_off = AR.alloc(2048)
            if xcore:
                @block.vector
                def _(v):
                    pg.wait(v, CC)
                    pg.inc(v.memset(MRK[:, 0:1], 0.0), VM)

                @block.sync
                def _(sy):
                    pg.wait(sy, CC)

                @block.scalar
                def _(s):
                    pg.wait(s, CC)
            cast_weights_to_dram(d_w_inT, WC_Z, 8, 1024, scrw_off,
                                 gain_cols=[26 + k for k in range(8)], msrc0=0)
            cast_weights_to_dram(d_w_inT, WC_Z + 8192, 8, 1024, scrw_off,
                                 gain_cols=[26 + k for k in range(8)], msrc0=1024)
            cast_weights_to_dram(d_w_outT, WC_OUT, 16, 1024, scrw_off,
                                 gain_cols=[34 + k for k in range(16)])
            for half in range(4):
                cast_weights_to_dram(d_w_gateT, WC_GATE + half * 8192, 8, 1024, scrw_off,
                                     gain_cols=[50 + k for k in range(8)],
                                     msrc0=half * 1024)
                cast_weights_to_dram(d_w_upT, WC_UP + half * 8192, 8, 1024, scrw_off,
                                     gain_cols=[50 + k for k in range(8)],
                                     msrc0=half * 1024)
            cast_weights_to_dram(d_w_downT, WC_DOWN, 32, 1024, scrw_off)

            AR.release()

            if xcore:
                @block.sync
                def _(sy):
                    pg.wait(sy, CC)
                    for j in range(8):
                        pg.inc(sy.dma_start(out=SCR_A[:, 16 + j:17 + j],
                                            in_=d_ssd_all[j, 0:32, 2048:2049],
                                            allow_slow_non_contiguous=True), DMA_IN, 16)

                @block.vector
                def _(v):
                    pg.wait(v, DMA_IN)
                    for arm in v.switch_core_id(n=128):
                        k = arm.logical % W
                        for j in range(8):
                            if j >= k:
                                v.memset(SCR_A[:, 24 + j:25 + j], -1.0e30)
                        if k >= 1:
                            v.memset(SCR_A[:, 24 + k - 1:24 + k], 0.0)
                        for j in range(k - 2, -1, -1):
                            v.drain()
                            v.tensor_tensor(SCR_A[:, 24 + j:25 + j],
                                            SCR_A[:, 24 + j + 1:24 + j + 2],
                                            SCR_A[:, 16 + j + 1:16 + j + 2], ALU.add)
                    pg.inc(v.memset(MRK[:, 7:8], 0.0), VM)

                @block.scalar
                def _(s):
                    pg.wait(s, VM)
                    pg.inc(s.activation(SCR_A[:, 24:32], SCR_A[:, 24:32], AF.Exp), SM)

                @block.vector
                def _(v):
                    pg.wait(v, SM)
                    pg.inc(v.tensor_copy(
                        SCR_A[:, 40:56],
                        ACS[:].rearrange("p (a b) -> p a b", b=128)[:, :, 127:128]), VM)

                @block.scalar
                def _(s):
                    pg.wait(s, VM)
                    pg.inc(s.activation(SCR_A[:, 40:56], SCR_A[:, 40:56], AF.Exp), SM)

                @block.vector
                def _(v):
                    pg.wait(v, SM)
                    pg.inc(v.memset(MRK[:, 6:7], 0.0), VM)

                @block.sync
                def _(sy):
                    pg.wait(sy, VM)
                    pg.wait(sy, SM)
                    pg.inc(sy.dma_start(out=d_acsflat[4:5, 0:256], in_=SCR_A[:, 24:32]),
                           DMA_SP, 16)
                    pg.inc(sy.dma_start(out=d_acsflat[4:5, 256:768], in_=SCR_A[:, 40:56]),
                           DMA_SP, 16)
                    pg.wait(sy, DMA_SP)
                    pg.inc(sy.dma_start(out=GS[:],
                                        in_=d_acsflat[4, 0:256].unsqueeze(0)
                                        .broadcast_to([128, 256])), DMA_SP, 16)
                    pg.inc(sy.dma_start(out=GAMC[:],
                                        in_=d_acsflat[4, 256:768].unsqueeze(0)
                                        .broadcast_to([128, 512])), DMA_SP, 16)

                @block.vector
                def _(v):
                    pg.wait(v, DMA_SP)
                    pg.inc(v.memset(HIN[:], 0.0), VM)

                for j in range(7):
                    for hh in range(2):
                        @block.sync
                        def _(sy, j=j, hh=hh):
                            pg.wait(sy, VM)
                            pg.inc(sy.dma_start(out=SJ[:],
                                                in_=d_ssd_all[j, :, hh * 1024:
                                                              (hh + 1) * 1024]), DMA_IN, 16)

                        @block.vector
                        def _(v, j=j, hh=hh):
                            pg.wait(v, DMA_IN)
                            for h2 in range(16):
                                h = hh * 16 + h2
                                v.scalar_tensor_tensor(HIN[:, h * 64:(h + 1) * 64],
                                                       SJ[:, h2 * 64:(h2 + 1) * 64],
                                                       GS[:, h * 8 + j:h * 8 + j + 1],
                                                       HIN[:, h * 64:(h + 1) * 64],
                                                       ALU.mult, ALU.add)
                            pg.inc(v.memset(MRK[:, 7:8], 0.0), VM)
            else:
                @block.vector
                def _(v):
                    pg.wait(v, VM)
                    pg.inc(v.memset(HIN[:], 0.0), VM)

                @block.vector
                def _(v):
                    pg.wait(v, VM)
                    pg.inc(v.tensor_copy(
                        SCR_A[:, 40:56],
                        ACS[:].rearrange("p (a b) -> p a b", b=128)[:, :, 127:128]), VM)

                @block.scalar
                def _(s):
                    pg.wait(s, VM)
                    pg.inc(s.activation(SCR_A[:, 40:56], SCR_A[:, 40:56], AF.Exp), SM)

                @block.sync
                def _(sy):
                    pg.wait(sy, VM)
                    pg.wait(sy, SM)
                    pg.inc(sy.dma_start(out=d_acsflat[4:5, 256:768], in_=SCR_A[:, 40:56]),
                           DMA_SP, 16)
                    pg.wait(sy, DMA_SP)
                    pg.inc(sy.dma_start(out=GAMC[:],
                                        in_=d_acsflat[4, 256:768].unsqueeze(0)
                                        .broadcast_to([128, 512])), DMA_SP, 16)

            if dbg and xcore:
                @block.gpsimd
                def _(g):
                    pg.wait_all(g, (VM, SM, DMA_SP, DMA_IN))
                    pg.inc(g.dma_start(out=dbg["hin"], in_=HIN[:]), DMA_G, 16)
                    pg.inc(g.dma_start(out=dbg["gs"], in_=GS[:]), DMA_G, 16)
                    pg.inc(g.dma_start(out=dbg["suf"], in_=SCR_A[:, 16:64]), DMA_G, 16)
                    pg.wait(g, DMA_G)

            WZ = AR.b16(wz_off, 2048)
            WO = AR.b16(wo_off, 2048)
            SQB = AR.b16(sq_off, 512)
            for s4 in range(4):
                c0 = RSUBS[s4]
                ln_stats_sub(c0, extra_rms=True, eps=EPS0)

                @block.tensor
                def _(t):
                    pg.wait(t, VM)
                    pg.inc(t.matmul(pbk[7][:], ONESR[:], rowb(B_NMS), start=True,
                                    stop=True), TM)

                @block.vector
                def _(v):
                    pg.wait(v, TM)
                    for f in range(8):
                        pg.inc(v.tensor_tensor(XBS[:, f * TSUB:(f + 1) * TSUB],
                                               XBS[:, f * TSUB:(f + 1) * TSUB],
                                               pbk[7][:], ALU.add), VM)

                @block.vector
                def _(v):
                    pg.wait(v, VM)
                    v.tensor_tensor(rows(R_T1), rows(R_M), rows(R_RS), ALU.mult)
                    v.tensor_copy(rowb(B_T1), rows(R_RS))
                    v.drain()
                    pg.inc(v.tensor_copy(rowb(B_T2), rows(R_T1)), VM)

                @block.tensor
                def _(t):
                    pg.wait(t, VM)
                    pg.inc(t.matmul(pbk[6][:], ONESR[:], rowb(B_T1), start=True, stop=True),
                           TM)
                    pg.inc(t.matmul(pbk[5][:], ONESR[:], rowb(B_T2), start=True, stop=True),
                           TM)

                @block.scalar
                def _(s):
                    pg.wait(s, TM)
                    s.activation(RS0BC[:], pbk[6][:], AF.Copy)
                    pg.inc(s.activation(MRS0BC[:], pbk[5][:], AF.Copy), SM)

                @block.sync
                def _(sy, s4=s4):
                    pg.wait(sy, DMA_SP)
                    pg.wait(sy, VM)
                    pg.inc(sy.dma_start(
                        out=Y1M[:].rearrange("p (f t) -> p f t", t=TSUB),
                        in_=d_y1sp.rearrange("p (f t) -> p f t", t=TLOC)
                        [:, :, s4 * TSUB:(s4 + 1) * TSUB]), DMA_IN, 16)

                for ci in range(4):
                    c = s4 * 4 + ci
                    if c > 0:
                        @block.vector
                        def _(v, c=c):
                            pg.wait(v, VM)
                            pg.wait(v, TM)
                            pg.wait(v, DMA_SP)
                            for h in range(NH):
                                v.tensor_scalar(HIN[:, h * 64:(h + 1) * 64],
                                                HIN[:, h * 64:(h + 1) * 64],
                                                GAMC[:, h * 16 + c - 1:h * 16 + c], None,
                                                ALU.mult)
                            pg.inc(v.memset(MRK[:, 7:8], 0.0), VM)

                    @block.sync
                    def _(sy, c=c):
                        pg.wait(sy, VM)
                        pg.inc(sy.dma_start(out=d_acsflat[2 + c % 2:3 + c % 2, 0:4096],
                                            in_=ACS[:, c * 128:(c + 1) * 128]), DMA_SP, 16)
                        pg.wait(sy, VM, v=crep2_guard[0])
                        pg.inc(sy.dma_start(
                            out=CREP2[:],
                            in_=CF[:, 3 + c * 128:3 + c * 128 + 128]
                            .unsqueeze(1).broadcast_to([128, 4, 128])), DMA_SP, 16)

                    for qd in range(8):
                        h0 = qd * 4

                        @block.sync
                        def _(sy, c=c, qd=qd, h0=h0):
                            pg.wait(sy, DMA_SP)
                            pg.wait(sy, SM, v=acse2_guard[0])
                            pg.wait(sy, VM, v=acse2_guard[1])
                            pg.inc(sy.dma_start(
                                out=ACSE2[:],
                                in_=d_acsflat[2 + c % 2, h0 * 128:(h0 + 4) * 128]
                                .unsqueeze(0).broadcast_to([128, 512])), DMA_SP, 16)

                        @block.scalar
                        def _(s, qd=qd):
                            pg.wait(s, DMA_SP)
                            pg.inc(s.activation(ACSE2[:], ACSE2[:], AF.Exp), SM)
                        acse2_guard[0] = pg.val(SM)

                        @block.vector
                        def _(v, qd=qd):
                            pg.wait(v, SM)
                            pg.wait(v, TM)
                            pg.inc(v.tensor_tensor(
                                CTL2[:, (qd % 2) * 512:(qd % 2) * 512 + 512],
                                CREP2[:], ACSE2[:], ALU.mult), VM)
                        acse2_guard[1] = pg.val(VM)
                        crep2_guard[0] = pg.val(VM)

                        if qd % 2 == 1:
                            g = qd // 2

                            @block.vector
                            def _(v, g=g):
                                pg.wait(v, VM)
                                pg.inc(v.tensor_copy(HCB[:], HIN[:, g * 512:(g + 1) * 512]),
                                       VM)

                            @block.tensor
                            def _(t, g=g):
                                pg.wait(t, VM)
                                b = banks.take(t)
                                for j in range(8):
                                    pr = (j % 2) * 64
                                    sg = (j // 2) * 128
                                    tp = (0, 64) if pr else None
                                    mm = t.matmul(b[pr:pr + 64, sg:sg + 128],
                                                  HCB[:, j * 64:(j + 1) * 64],
                                                  CTL2[:, (j // 4) * 512 + (j % 4) * 128:
                                                       (j // 4) * 512 + (j % 4) * 128 + 128],
                                                  start=True, stop=True, tile_position=tp)
                                pg.inc(mm, TM)

                            bnkC = banks.banks[(banks.i - 1) % len(banks.banks)]

                            @block.vector
                            def _(v, g=g, ci=ci, b=bnkC):
                                pg.wait(v, TM)
                                pg.wait(v, DMA_IN)
                                yv = Y1M[:].rearrange("p (f t) -> p f t", t=TSUB)
                                pg.inc(v.tensor_tensor(
                                    yv[:, g * 4:(g + 1) * 4, ci * 128:(ci + 1) * 128],
                                    yv[:, g * 4:(g + 1) * 4, ci * 128:(ci + 1) * 128],
                                    b[0:128, 0:512].rearrange("p (f t) -> p f t", t=128),
                                    ALU.add), VM)
                                banks.consumed(b, VM, pg.val(VM))

                for m in range(16):
                    @block.sync
                    def _(sy, m=m):
                        pg.wait(sy, TM)
                        half = m // 8
                        pg.inc(sy.dma_start(out=WZ[0:128, 0:1024]
                                            .rearrange("p (k a) -> p k a", a=128),
                                            in_=d_wcast[:, WC_Z + half * 8192:
                                                        WC_Z + (half + 1) * 8192]
                                            .rearrange("p (k a) -> p k a", a=1024)
                                            [:, :, (m % 8) * 128:(m % 8 + 1) * 128]),
                               DMA_IN, 16)

                    @block.tensor
                    def _(t, m=m):
                        pg.wait(t, DMA_IN)
                        pg.wait(t, VM)
                        b = banks.take(t)
                        for k in range(8):
                            mm = t.matmul(b[0:128, :],
                                          WZ[0:128, k * 128:(k + 1) * 128],
                                          XBS[:, k * TSUB:(k + 1) * TSUB],
                                          start=(k == 0), stop=(k == 7))
                        pg.inc(mm, TM)

                    bnkZ = banks.banks[(banks.i - 1) % len(banks.banks)]

                    @block.scalar
                    def _(s, m=m, b=bnkZ):
                        pg.wait(s, TM)
                        pg.wait(s, VM)
                        pg.inc(s.activation(EVZ[:], b[0:128, :], AF.Silu), SM)
                        banks.consumed(b, SM, pg.val(SM))

                    @block.vector
                    def _(v, m=m):
                        pg.wait(v, SM)
                        pg.inc(v.tensor_tensor(Y1M[:, m * TSUB:(m + 1) * TSUB],
                                               Y1M[:, m * TSUB:(m + 1) * TSUB],
                                               EVZ[:], ALU.mult), VM)

                for m in range(16):
                    @block.scalar
                    def _(s, m=m):
                        pg.wait(s, VM)
                        pg.wait(s, TM)
                        pg.inc(s.activation(SQB[0:128, 0:512],
                                            Y1M[:, m * TSUB:(m + 1) * TSUB], AF.Square), SM)

                    @block.tensor
                    def _(t, m=m):
                        pg.wait(t, SM)
                        pg.inc(t.matmul(pbk[5][0:1, :], ONESC[:], SQB[0:128, 0:512],
                                        start=(m == 0), stop=(m == 15)), TM)

                @block.vector
                def _(v):
                    pg.wait(v, TM)
                    pg.inc(v.tensor_scalar(rows(R_T2), pbk[5][0:1, :], 1.0 / E, None,
                                           ALU.mult), VM)

                @block.scalar
                def _(s):
                    pg.wait(s, VM)
                    s.activation(rows(R_T2), rows(R_T2), AF.Ln, bias=EPSM)
                    pg.inc(s.activation(rows(R_T2), rows(R_T2), AF.Exp, scale=-0.5), SM)

                @block.vector
                def _(v):
                    pg.wait(v, SM)
                    pg.inc(v.tensor_copy(rowb(B_T1), rows(R_T2)), VM)

                @block.tensor
                def _(t):
                    pg.wait(t, VM)
                    pg.inc(t.matmul(pbk[6][:], ONESR[:], rowb(B_T1), start=True, stop=True),
                           TM)

                @block.scalar
                def _(s):
                    pg.wait(s, TM)
                    pg.inc(s.activation(RSMBC[:], pbk[6][:], AF.Copy), SM)

                for m in range(8):
                    @block.sync
                    def _(sy, m=m):
                        pg.wait(sy, TM)
                        pg.inc(sy.dma_start(out=WO[0:128, 0:2048]
                                            .rearrange("p (k a) -> p k a", a=128),
                                            in_=d_wcast[:, WC_OUT:WC_OUT + 16384]
                                            .rearrange("p (k a) -> p k a", a=1024)
                                            [:, :, m * 128:(m + 1) * 128]), DMA_IN, 16)

                    @block.tensor
                    def _(t, m=m):
                        pg.wait(t, DMA_IN)
                        pg.wait(t, VM)
                        pg.wait(t, SM)
                        b = banks.take(t)
                        for k in range(16):
                            mm = t.matmul(b[0:128, :],
                                          WO[0:128, k * 128:(k + 1) * 128],
                                          Y1M[:, k * TSUB:(k + 1) * TSUB],
                                          start=(k == 0), stop=(k == 15))
                        pg.inc(mm, TM)

                    bnkO = banks.banks[(banks.i - 1) % len(banks.banks)]

                    @block.vector
                    def _(v, m=m, s4=s4, b=bnkO):
                        pg.wait(v, TM)
                        cc = RSUBS[s4]
                        xs = X[:, m * TX + cc:m * TX + cc + TSUB]
                        v.tensor_tensor(EV1[:], b[0:128, :], RSMBC[:], ALU.mult)
                        v.tensor_tensor(EV2[:], xs, RS0BC[:], ALU.mult)
                        v.tensor_tensor(EV2[:], EV2[:], MRS0BC[:], ALU.subtract)
                        v.tensor_tensor(EV2[:], EV2[:], EV1[:], ALU.add)
                        v.tensor_copy(HFULL[0:128, m * TLOC + s4 * TSUB:
                                            m * TLOC + (s4 + 1) * TSUB], EV2[:])
                        pg.inc(v.tensor_tensor(xs, xs, EV2[:], ALU.add), VM)
                        banks.consumed(b, VM, pg.val(VM))

            if dbg:
                @block.gpsimd
                def _(g):
                    pg.wait_all(g, (VM, SM, TM))
                    pg.inc(g.dma_start(out=dbg["hid"], in_=HFULL[0:128, 0:8 * TLOC]), DMA_G, 16)
                    pg.inc(g.dma_start(out=dbg["xg"], in_=X[:]), DMA_G, 16)
                    pg.inc(g.dma_start(out=dbg["y1m"], in_=Y1M[:]), DMA_G, 16)
                    if xcore:
                        pg.inc(g.dma_start(out=dbg["gamc"], in_=GAMC[:]), DMA_G, 16)
                        pg.inc(g.dma_start(out=dbg["hcb"], in_=HCB[:]), DMA_G, 16)
                        pg.inc(g.dma_start(out=dbg["ctl2"], in_=CTL2[:]), DMA_G, 16)
                        pg.inc(g.dma_start(out=dbg["acse2"], in_=ACSE2[:]), DMA_G, 16)
                        pg.inc(g.dma_start(out=dbg["hin2"], in_=HIN[:]), DMA_G, 16)
                    pg.wait(g, DMA_G)
            AR.release()

            AR.mark()
            gu_off = AR.alloc(8192)
            wgu_off = AR.alloc(4096)
            dws_off = AR.alloc(2048)
            gsil_off = AR.alloc(256)
            sq2_off = AR.alloc(512)
            GUF = AR.b16(gu_off, 32 * TSUB)
            WGU = AR.b16(wgu_off, 8192)
            DWS = AR.b16(dws_off, 4096)
            GSIL = AR.b16(gsil_off, TSUB)
            SQ2 = AR.b16(sq2_off, 1024)

            for s4 in range(4):
                for f in range(8):
                    @block.scalar
                    def _(s, f=f, s4=s4):
                        pg.wait(s, VM)
                        pg.wait(s, TM)
                        pg.inc(s.activation(SQ2[0:128, (f % 2) * 512:(f % 2) * 512 + 512],
                                            HFULL[0:128, f * TLOC + s4 * TSUB:
                                                  f * TLOC + (s4 + 1) * TSUB],
                                            AF.Square), SM)

                    @block.tensor
                    def _(t, f=f):
                        pg.wait(t, SM)
                        pg.inc(t.matmul(pbk[5][0:1, :], ONESC[:],
                                        SQ2[0:128, (f % 2) * 512:(f % 2) * 512 + 512],
                                        start=(f == 0), stop=(f == 7)), TM)

                @block.vector
                def _(v):
                    pg.wait(v, TM)
                    pg.inc(v.tensor_scalar(rows(R_T2), pbk[5][0:1, :], 1.0 / H, None,
                                           ALU.mult), VM)

                @block.scalar
                def _(s):
                    pg.wait(s, VM)
                    s.activation(rows(R_T2), rows(R_T2), AF.Ln, bias=EPS1)
                    pg.inc(s.activation(rows(R_T2), rows(R_T2), AF.Exp, scale=-0.5), SM)

                @block.vector
                def _(v):
                    pg.wait(v, SM)
                    pg.inc(v.tensor_copy(rowb(B_T1), rows(R_T2)), VM)

                @block.tensor
                def _(t):
                    pg.wait(t, VM)
                    pg.inc(t.matmul(pbk[7][:], ONESR[:], rowb(B_T1), start=True, stop=True),
                           TM)

                @block.vector
                def _(v, s4=s4):
                    pg.wait(v, TM)
                    for f in range(8):
                        pg.inc(v.tensor_tensor(HFULL[0:128, f * TLOC + s4 * TSUB:
                                                     f * TLOC + (s4 + 1) * TSUB],
                                               HFULL[0:128, f * TLOC + s4 * TSUB:
                                                     f * TLOC + (s4 + 1) * TSUB],
                                               pbk[7][:], ALU.mult), VM)

                def hsub(f, s4=s4):
                    return HFULL[0:128, f * TLOC + s4 * TSUB:f * TLOC + (s4 + 1) * TSUB]

                for grp in range(8):
                    @block.sync
                    def _(sy, grp=grp):
                        pg.wait(sy, TM)
                        pg.inc(sy.dma_start(out=WGU[0:128, 0:4096]
                                            .rearrange("p (k a) -> p k a", a=512),
                                            in_=d_wcast[:, WC_GATE + (grp // 2) * 8192:
                                                    WC_GATE + (grp // 2) * 8192 + 8192]
                                            .rearrange("p (k a) -> p k a", a=1024)
                                            [:, :, (grp % 2) * 512:(grp % 2) * 512 + 512]),
                               DMA_IN, 16)
                        pg.inc(sy.dma_start(out=WGU[0:128, 4096:8192]
                                            .rearrange("p (k a) -> p k a", a=512),
                                            in_=d_wcast[:, WC_UP + (grp // 2) * 8192:
                                                    WC_UP + (grp // 2) * 8192 + 8192]
                                            .rearrange("p (k a) -> p k a", a=1024)
                                            [:, :, (grp % 2) * 512:(grp % 2) * 512 + 512]),
                               DMA_IN, 16)

                    for mt in range(4):
                        @block.tensor
                        def _(t, grp=grp, mt=mt, s4=s4):
                            pg.wait(t, DMA_IN)
                            pg.wait(t, VM)
                            b = banks.take(t)
                            for k in range(8):
                                mm = t.matmul(b[0:128, :],
                                              WGU[0:128, k * 512 + mt * 128:
                                                  k * 512 + (mt + 1) * 128],
                                              hsub(k), start=(k == 0), stop=(k == 7))
                            pg.inc(mm, TM)

                        bnkG = banks.banks[(banks.i - 1) % len(banks.banks)]

                        @block.scalar
                        def _(s, b=bnkG):
                            pg.wait(s, TM)
                            pg.wait(s, VM)
                            pg.inc(s.activation(GSIL[:], b[0:128, :], AF.Silu), SM)
                            banks.consumed(b, SM, pg.val(SM))

                        @block.tensor
                        def _(t, grp=grp, mt=mt, s4=s4):
                            pg.wait(t, SM)
                            b = banks.take(t)
                            for k in range(8):
                                mm = t.matmul(b[0:128, :],
                                              WGU[0:128, 4096 + k * 512 + mt * 128:
                                                  4096 + k * 512 + (mt + 1) * 128],
                                              hsub(k), start=(k == 0), stop=(k == 7))
                            pg.inc(mm, TM)

                        bnkU = banks.banks[(banks.i - 1) % len(banks.banks)]

                        @block.vector
                        def _(v, grp=grp, mt=mt, b=bnkU):
                            pg.wait(v, TM)
                            pg.inc(v.tensor_tensor(GUF[0:128, (grp * 4 + mt) * TSUB:
                                                       (grp * 4 + mt + 1) * TSUB],
                                                   GSIL[:], b[0:128, :], ALU.mult), VM)
                            banks.consumed(b, VM, pg.val(VM))

                for m in range(8):
                    @block.sync
                    def _(sy, m=m):
                        pg.wait(sy, TM)
                        pg.inc(sy.dma_start(out=DWS[0:128, 0:4096]
                                            .rearrange("p (k a) -> p k a", a=128),
                                            in_=d_wcast[:, WC_DOWN:WC_DOWN + 32768]
                                            .rearrange("p (k a) -> p k a", a=1024)
                                            [:, :, m * 128:(m + 1) * 128]), DMA_IN, 16)

                    @block.tensor
                    def _(t, m=m):
                        pg.wait(t, DMA_IN)
                        pg.wait(t, VM)
                        b = banks.take(t)
                        for k in range(32):
                            mm = t.matmul(b[0:128, :],
                                          DWS[0:128, k * 128:(k + 1) * 128],
                                          GUF[0:128, k * TSUB:(k + 1) * TSUB],
                                          start=(k == 0), stop=(k == 31))
                        pg.inc(mm, TM)

                    bnkD = banks.banks[(banks.i - 1) % len(banks.banks)]

                    @block.vector
                    def _(v, m=m, s4=s4, b=bnkD):
                        pg.wait(v, TM)
                        cc = RSUBS[s4]
                        xs = X[:, m * TX + cc:m * TX + cc + TSUB]
                        pg.inc(v.tensor_tensor(xs, xs, b[0:128, :], ALU.add), VM)
                        banks.consumed(b, VM, pg.val(VM))

            AR.release()
            AR.release()


        def emit_attn_b():
            AR.mark()
            xpb_off = AR.alloc(16388)
            XPB = AR.b16(xpb_off, 2 * 16388)

            def xpb(dt_, c0, wid):
                return XPB[:, dt_ * 16387 + c0:dt_ * 16387 + c0 + wid]

            idb2_off = AR.alloc(64)
            IDB2 = AR.b16(idb2_off, 128)
            abt_off = AR.alloc(32)
            ABT = AR.f32(abt_off, 32)
            wxb_off = AR.alloc(48)
            WXB = AR.b16(wxb_off, 96)
            carry_off = AR.alloc(32)
            CARRY = AR.f32(carry_off, 32)
            wb_off = AR.alloc(2048)
            WBSL = AR.b16(wb_off, 8 * 512)

            AR.mark()
            scr_off = AR.alloc(1024)
            scr = AR.f32(scr_off, 1024)

            @block.sync
            def _(sy):
                pg.wait(sy, VM)
                pg.wait(sy, SM)
                pg.wait(sy, TM)
                pg.inc(sy.dma_start(out=scr[:, 0:16], in_=d_bAlog[0:128, :]), DMA_IN, 16)
                pg.inc(sy.dma_start(out=scr[:, 16:32], in_=d_bAlog[128:256, :]), DMA_IN, 16)
                pg.inc(sy.dma_start(out=scr[:, 32:65], in_=d_bWx[0:128, :]), DMA_IN, 16)
                pg.inc(sy.dma_start(out=scr[:, 65:98], in_=d_bWx[128:256, :]), DMA_IN, 16)

            @block.scalar
            def _(s):
                pg.wait(s, DMA_IN)
                pg.inc(s.activation(ABT[:], scr[:, 0:32], AF.Exp), SM)

            @block.vector
            def _(v):
                pg.wait(v, SM)
                v.tensor_scalar(ABT[:], ABT[:], -1.0, None, ALU.mult)
                v.tensor_copy(WXB[:, 0:33], scr[:, 32:65])
                v.tensor_copy(WXB[:, 33:66], scr[:, 65:98])
                v.memset(XPB[:, 0:3], 0.0)
                v.memset(XPB[:, 16387:16390], 0.0)
                pg.inc(v.memset(CARRY[:], 0.0), VM)

            for s4 in range(4):
                c0 = RSUBS[s4]

                @block.vector
                def _(v):
                    pg.wait(v, DMA_SP)

                @block.scalar
                def _(s):
                    pg.wait(s, DMA_SP)

                ln_stats_sub(c0, extra_rms=False, eps=EPS0)

                @block.tensor
                def _(t):
                    pg.wait(t, VM)
                    pg.inc(t.matmul(pbk[7][:], ONESR[:], rowb(B_NMS), start=True,
                                    stop=True), TM)

                @block.vector
                def _(v):
                    pg.wait(v, TM)
                    for f in range(8):
                        pg.inc(v.tensor_tensor(XBS[:, f * TSUB:(f + 1) * TSUB],
                                               XBS[:, f * TSUB:(f + 1) * TSUB],
                                               pbk[7][:], ALU.add), VM)

                @block.scalar
                def _(s, s4=s4):
                    pg.wait(s, VM)
                    pg.wait(s, TM)
                    pg.inc(s.dma_start(
                        out=d_f3_pub.rearrange("(f p) t -> p f t", p=128)
                        [:, :, s4 * TSUB:(s4 + 1) * TSUB],
                        in_=XBS[:].rearrange("p (f t) -> p f t", t=TSUB)), DMA_SP, 16)

            @block.gpsimd
            def _(g):
                pg.wait(g, DMA_SP)
                if dbg:
                    pg.inc(g.dma_start(out=dbg["f3"], in_=d_f3_pub[:, :]), DMA_G, 16)
                    pg.wait(g, DMA_G)
                pg.inc(g.collective_compute("AllGather", ALU.bypass, replica_groups=rg,
                                            ins=[d_f3_pub], outs=[d_f3_all]), CC)

            load_weight_bf16(d_bWin, WBSL, 8, 512, scr_off)

            @block.sync
            def _(sy):
                pg.wait(sy, CC)

            for sec in range(NSEC):
                j = sec // 4
                cc = (sec % 4) * TSUB

                @block.sync
                def _(sy, j=j, cc=cc):
                    pg.wait(sy, TM)
                    pg.inc(sy.dma_start(
                        out=XBS[:].rearrange("p (f t) -> p f t", t=TSUB),
                        in_=d_f3_all[j].rearrange("(f p) t -> p f t", p=128)
                        [:, :, cc:cc + TSUB]), DMA_IN, 16)

                for mi in range(2):
                    @block.tensor
                    def _(t, mi=mi):
                        pg.wait(t, DMA_IN)
                        pg.wait(t, VM)
                        pg.wait(t, SM)
                        b = banks.take(t)
                        for k in range(8):
                            mm = t.matmul(b[0:128, :],
                                          WBSL[:, k * 512 + mi * 128:k * 512 + (mi + 1) * 128],
                                          XBS[:, k * TSUB:(k + 1) * TSUB],
                                          start=(k == 0), stop=(k == 7))
                        pg.inc(mm, TM)

                    bnk = banks.banks[(banks.i - 1) % len(banks.banks)]

                    @block.scalar
                    def _(s, mi=mi, sec=sec, b=bnk):
                        pg.wait(s, TM)
                        pg.inc(s.activation(xpb(mi, 3 + sec * TSUB, TSUB),
                                            b[0:128, :], AF.Copy), SM)
                        banks.consumed(b, SM, pg.val(SM))

            if dbg:
                @block.gpsimd
                def _(g):
                    pg.wait_all(g, (VM, SM, TM, DMA_SP, DMA_IN))
                    for dt_ in range(2):
                        pg.inc(g.dma_start(out=dbg["bu2"][dt_ * 128:(dt_ + 1) * 128, :],
                                           in_=xpb(dt_, 3, L)), DMA_G, 16)
                    pg.wait(g, DMA_G)

            for sec in reversed(range(NSEC)):
                for dt_ in range(2):
                    @block.vector
                    def _(v, sec=sec, dt_=dt_):
                        pg.wait(v, SM)
                        pg.wait(v, DMA_SP)
                        lo = 3 + sec * TSUB
                        v.tensor_scalar(EV1[:], xpb(dt_, lo - 3, TSUB),
                                        CONVW[:, 72 + dt_ * 4:73 + dt_ * 4], None, ALU.mult)
                        for kk in range(1, 4):
                            v.scalar_tensor_tensor(EV1[:], xpb(dt_, lo - 3 + kk, TSUB),
                                                   CONVW[:, 72 + dt_ * 4 + kk:
                                                         73 + dt_ * 4 + kk],
                                                   EV1[:], ALU.mult, ALU.add)
                        pg.inc(v.memset(MRK[:, 0:1], 0.0), VM)

                    @block.scalar
                    def _(s, sec=sec, dt_=dt_):
                        pg.wait(s, VM)
                        pg.inc(s.activation(xpb(dt_, 3 + sec * TSUB, TSUB), EV1[:],
                                            AF.Silu, bias=CONVB[:, 18 + dt_:19 + dt_]), SM)

            for sec in range(NSEC):
                @block.tensor
                def _(t, sec=sec):
                    pg.wait(t, SM)
                    pg.wait(t, VM)
                    b = banks.take(t)
                    for kt in range(2):
                        mm = t.matmul(b[0:33, :], WXB[:, kt * 33:(kt + 1) * 33],
                                      xpb(kt, 3 + sec * TSUB, TSUB),
                                      start=(kt == 0), stop=(kt == 1))
                    pg.inc(mm, TM)

                bnk = banks.banks[(banks.i - 1) % len(banks.banks)]

                @block.vector
                def _(v, sec=sec, b=bnk):
                    pg.wait(v, TM)
                    pg.wait(v, DMA_SP)
                    pg.inc(v.tensor_copy(EV1[0:33, 0:TSUB], b[0:33, :]), VM)
                    banks.consumed(b, VM, pg.val(VM))

                @block.scalar
                def _(s, sec=sec):
                    pg.wait(s, VM)
                    pg.inc(s.dma_start(out=d_xdbl_pub[:, sec * TSUB:(sec + 1) * TSUB],
                                       in_=EV1[0:33, 0:TSUB]), DMA_SP, 16)

            @block.gpsimd
            def _(g):
                pg.wait(g, DMA_SP)
                pg.inc(g.collective_compute("AllReduce", ALU.add, replica_groups=rg,
                                            ins=[d_xdbl_pub], outs=[d_xdbl_all]), CC)

            dl_off = AR.alloc(384)
            DLT = AR.f32(dl_off, 128)
            DSC1 = AR.f32(dl_off + 128, 128)
            DSC2 = AR.f32(dl_off + 256, 128)

            @block.sync
            def _(sy):
                pg.wait(sy, CC)
                pg.inc(sy.dma_start(out=DLT[:],
                                    in_=d_xdbl_all[0, :].rearrange("(p a) -> p a", p=128)),
                       DMA_IN, 16)

            @block.scalar
            def _(s):
                pg.wait(s, DMA_IN)
                s.activation(DSC1[:], DLT[:], AF.Abs)
                s.activation(DSC1[:], DSC1[:], AF.Exp, scale=-1.0)
                s.activation(DSC1[:], DSC1[:], AF.Ln, bias=1.0)
                pg.inc(s.activation(DSC2[:], DLT[:], AF.Relu), SM)

            @block.vector
            def _(v):
                pg.wait(v, SM)
                pg.inc(v.tensor_tensor(DLT[:], DSC1[:], DSC2[:], ALU.add), VM)

            @block.scalar
            def _(s):
                pg.wait(s, VM)
                pg.inc(s.dma_start(out=d_delta[0:1, :],
                                   in_=DLT[:].rearrange("p a -> (p a)").unsqueeze(0)
                                   if False else DLT[:]), DMA_SP, 16)

            for sec2 in range(NSEC):
                @block.sync
                def _(sy, sec2=sec2):
                    pg.wait(sy, VM)
                    pg.inc(sy.dma_start(out=EV1[0:33, 0:TSUB],
                                        in_=d_xdbl_all[:, sec2 * TSUB:(sec2 + 1) * TSUB]),
                           DMA_IN, 16)

                @block.vector
                def _(v, sec2=sec2):
                    pg.wait(v, DMA_IN)
                    pg.wait(v, DMA_SP)
                    pg.inc(v.tensor_copy(EV2[0:33, :].bitcast(BF16)[0:33, 0:TSUB],
                                         EV1[0:33, 0:TSUB]), VM)

                @block.scalar
                def _(s, sec2=sec2):
                    pg.wait(s, VM)
                    pg.inc(s.dma_start(out=d_xdblb[:, sec2 * TSUB:(sec2 + 1) * TSUB],
                                       in_=EV2[0:33, :].bitcast(BF16)[0:33, 0:TSUB]),
                           DMA_SP, 16)

            AR.mark()
            scrw_off = AR.alloc(2048)
            cast_weights_to_dram(d_bWout, WC_OUT, 16, 1024, scrw_off)
            AR.release()

            AR.release()

            AR.mark()
            bbc_off = AR.alloc(1024)
            cbc_off = AR.alloc(1024)
            du_off = AR.alloc(512)
            xdd_off = AR.alloc(512)
            dab_off = AR.alloc(512)
            ib_off = AR.alloc(512)
            so_off = AR.alloc(512)
            hb_off = AR.alloc(256)
            hc2_off = AR.alloc(256)
            zrhs_off = AR.alloc(2048)
            BBC = AR.b16(bbc_off, 2048)
            CBC = AR.b16(cbc_off, 2048)
            DU = AR.b16(du_off, 1024)
            DELTF = AR.f32(xdd_off, 512)
            DAB2 = [AR.f32(dab_off, 512)] * 2
            IB2 = [AR.f32(ib_off, 512)] * 2
            SO2 = [AR.f32(so_off, 512)] * 2
            HB2 = [AR.b16(hb_off, 512)] * 2
            HC22 = [AR.b16(hc2_off, 512)] * 2
            DAB, IB, SO, HB, HC2 = DAB2[0], IB2[0], SO2[0], HB2[0], HC22[0]

            ZRHS = AR.b16(zrhs_off, 4096)
            ZB = EV2[:].bitcast(BF16)[0:128, 0:TSUB]
            ZS = EV2[:].bitcast(BF16)[0:128, TSUB:2 * TSUB]
            @block.gpsimd
            def _(g):
                pg.wait(g, VM)
                pg.wait(g, SM)
                pg.inc(g.iota(DAB[:, 0:128], [[-1, 128]], base=0, channel_multiplier=1,
                              allow_small_or_imprecise_dtypes=True), GM)

            @block.vector
            def _(v):
                pg.wait(v, GM)
                pg.wait(v, DMA_SP)
                v.tensor_scalar(IB[:, 0:128], DAB[:, 0:128], 0.0, None, ALU.is_equal)
                v.drain()
                pg.inc(v.tensor_copy(IDB2[:], IB[:, 0:128]), VM)

            if dbg:
                @block.gpsimd
                def _(g):
                    pg.wait_all(g, (VM, SM, TM, DMA_SP, DMA_IN))
                    for dt_ in range(2):
                        pg.inc(g.dma_start(out=dbg["bu"][dt_ * 128:(dt_ + 1) * 128, :],
                                           in_=xpb(dt_, 3, L)), DMA_G, 16)
                    pg.inc(g.dma_start(out=dbg["bxd"], in_=d_xdbl_all[:, :]), DMA_G, 16)
                    pg.inc(g.dma_start(out=dbg["bdel"], in_=d_delta[:, :]), DMA_G, 16)
                    pg.wait(g, DMA_G)

            bbc_guard = [0]
            for sec in range(NSEC):
                sc = sec * TSUB

                @block.sync
                def _(sy, sec=sec, sc=sc):
                    pg.wait(sy, DMA_SP)
                    pg.wait(sy, VM, v=bbc_guard[0])
                    pg.inc(sy.dma_start(out=DELTF[:],
                                        in_=d_delta[0, sc:sc + TSUB].unsqueeze(0)
                                        .broadcast_to([128, TSUB])), DMA_IN, 16)

                @block.vector
                def _(v, sec=sec):
                    pg.wait(v, DMA_IN)
                    for dt_ in range(2):
                        pg.inc(v.tensor_tensor(DU[:, dt_ * TSUB:(dt_ + 1) * TSUB],
                                               xpb(dt_, 3 + sec * TSUB, TSUB),
                                               DELTF[:], ALU.mult), VM)

                for sq in range(4):
                    @block.sync
                    def _(sy, sec=sec, sc=sc, sq=sq):
                        pg.wait(sy, VM, v=bbc_guard[0])
                        pg.inc(sy.dma_start(
                            out=BBC[:].rearrange("p (a t) -> p a t", t=TSUB),
                            in_=d_xdblb[1 + sq * 4:1 + (sq + 1) * 4, sc:sc + TSUB]
                            .unsqueeze(0).broadcast_to([128, 4, TSUB])), DMA_IN, 16)
                        pg.inc(sy.dma_start(
                            out=CBC[:].rearrange("p (a t) -> p a t", t=TSUB),
                            in_=d_xdblb[17 + sq * 4:17 + (sq + 1) * 4, sc:sc + TSUB]
                            .unsqueeze(0).broadcast_to([128, 4, TSUB])), DMA_IN, 16)

                    for si in range(4):
                        s_ = sq * 4 + si
                        for dt_ in range(2):
                            pty = (s_ * 2 + dt_) % 2
                            DAB, IB, SO, HB, HC2 = (DAB2[pty], IB2[pty], SO2[pty],
                                                    HB2[pty], HC22[pty])

                            @block.scalar
                            def _(s, s_=s_, dt_=dt_, sec=sec, DAB=DAB):
                                pg.wait(s, DMA_IN)
                                pg.wait(s, VM)
                                pg.inc(s.activation(DAB[:], DELTF[:], AF.Exp,
                                                    scale=ABT[:, dt_ * 16 + s_:
                                                              dt_ * 16 + s_ + 1]), SM)

                            @block.vector
                            def _(v, s_=s_, si=si, dt_=dt_, sec=sec, DAB=DAB, IB=IB,
                                  SO=SO, HB=HB, HC2=HC2):
                                pg.wait(v, SM)
                                pg.wait(v, TM)
                                v.tensor_tensor(IB[:], DU[:, dt_ * TSUB:(dt_ + 1) * TSUB],
                                                BBC[:, si * TSUB:(si + 1) * TSUB], ALU.mult)
                                v.drain()
                                v.tensor_tensor_scan(SO[:], DAB[:], IB[:],
                                                     CARRY[:, dt_ * 16 + s_:
                                                           dt_ * 16 + s_ + 1],
                                                     ALU.mult, ALU.add)
                                v.drain()
                                v.tensor_copy(CARRY[:, dt_ * 16 + s_:dt_ * 16 + s_ + 1],
                                              SO[:, TSUB - 1:TSUB])
                                v.tensor_copy(HB[:], SO[:])
                                v.drain()
                                pg.inc(v.tensor_tensor(HC2[:], HB[:],
                                                       CBC[:, si * TSUB:(si + 1) * TSUB],
                                                       ALU.mult), VM)

                            if dbg and sec == 0 and s_ == 0 and dt_ == 0:
                                @block.scalar
                                def _(s):
                                    pg.wait(s, VM)
                                    pg.inc(s.dma_start(out=dbg["sdab"], in_=DAB2[0][:]), DMA_G, 16)
                                    pg.inc(s.dma_start(out=dbg["sib"], in_=IB2[0][:]), DMA_G, 16)
                                    pg.inc(s.dma_start(out=dbg["sso"], in_=SO2[0][:]), DMA_G, 16)
                                    pg.inc(s.dma_start(out=dbg["shc"], in_=HC22[0][:]), DMA_G, 16)
                                    pg.inc(s.dma_start(out=dbg["sdu"],
                                                       in_=DU[:, 0:TSUB]), DMA_G, 16)
                                    pg.inc(s.dma_start(out=dbg["sbbc"],
                                                       in_=BBC[:, 0:TSUB]), DMA_G, 16)
                                    pg.inc(s.dma_start(out=dbg["scbc"],
                                                       in_=CBC[:, 0:TSUB]), DMA_G, 16)
                                    pg.inc(s.dma_start(out=dbg["shb"], in_=HB2[0][:]), DMA_G, 16)
                                    pg.wait(s, DMA_G)

                            if dbg and sec == 0 and dt_ == 0:
                                @block.scalar
                                def _(s, s_=s_):
                                    pg.wait(s, VM)
                                    pg.inc(s.dma_start(
                                        out=dbg["hcp"][:, s_ * TSUB:(s_ + 1) * TSUB],
                                        in_=HC22[0][:]), DMA_G, 16)
                                    pg.wait(s, DMA_G)

                            @block.tensor
                            def _(t, s_=s_, dt_=dt_, HC2=HC2):
                                pg.wait(t, VM)
                                pg.wait(t, SM)
                                bY = pbk[5 + dt_]
                                mm = t.matmul(bY[:], IDB2[:], HC2[:],
                                              start=(s_ == 0), stop=(s_ == 15))
                                pg.inc(mm, TM)
                    bbc_guard[0] = pg.val(VM)

                @block.sync
                def _(sy, sec=sec):
                    pg.wait(sy, VM)
                    pg.wait(sy, SM)
                    pg.wait(sy, TM)
                    j = sec // 4
                    cc = (sec % 4) * TSUB
                    pg.inc(sy.dma_start(
                        out=ZRHS[:].rearrange("p (f t) -> p f t", t=TSUB),
                        in_=d_f3_all[j].rearrange("(f p) t -> p f t", p=128)
                        [:, :, cc:cc + TSUB]), DMA_IN, 16)

                for dt_ in range(2):
                    @block.tensor
                    def _(t, dt_=dt_):
                        pg.wait(t, DMA_IN)
                        pg.wait(t, VM)
                        b = banks.take(t)
                        for k in range(8):
                            mm = t.matmul(b[0:128, :],
                                          WBSL[:, k * 512 + (2 + dt_) * 128:
                                               k * 512 + (3 + dt_) * 128],
                                          ZRHS[:, k * TSUB:(k + 1) * TSUB],
                                          start=(k == 0), stop=(k == 7))
                        pg.inc(mm, TM)

                    bnkZ2 = banks.banks[(banks.i - 1) % len(banks.banks)]

                    @block.scalar
                    def _(s, dt_=dt_, b=bnkZ2):
                        pg.wait(s, TM)
                        dst_z = ZB if dt_ == 0 else ZS
                        pg.inc(s.activation(dst_z, b[0:128, :], AF.Silu), SM)
                        banks.consumed(b, SM, pg.val(SM))

                if dbg and sec == 0:
                    @block.vector
                    def _(v):
                        pg.wait(v, TM)
                        pg.inc(v.tensor_copy(EV1[:], pbk[5][:]), VM)

                    @block.scalar
                    def _(s):
                        pg.wait(s, VM)
                        pg.inc(s.dma_start(out=dbg["yv0"], in_=EV1[:]), DMA_G, 16)
                        pg.wait(s, DMA_G)

                if dbg and sec == 0:
                    @block.scalar
                    def _(s):
                        pg.wait(s, DMA_IN)
                        pg.inc(s.dma_start(out=dbg["szraw"], in_=ZB[:, 0:TSUB]), DMA_G, 16)
                        pg.wait(s, DMA_G)

                for dt_ in range(2):
                    @block.scalar
                    def _(s, dt_=dt_, sec=sec):
                        pg.wait(s, DMA_IN)
                        pg.wait(s, VM)
                        pg.wait(s, TM)

                        if dbg and sec == 0 and dt_ == 0:
                            pg.inc(s.dma_start(out=dbg["szb"], in_=ZB[:, 0:TSUB]),
                                   DMA_G, 16)
                            s.wait_ge(pg._s(DMA_G)[0], pg.val(DMA_G))
                        pg.inc(s.activation(
                            EV1[:].bitcast(BF16)[0:128, dt_ * TSUB:(dt_ + 1) * TSUB],
                            pbk[5 + dt_][:], AF.Copy), SM)

                    @block.vector
                    def _(v, dt_=dt_, sec=sec):
                        pg.wait(v, SM)
                        src_z = ZB if dt_ == 0 else ZS
                        yev = EV1[:].bitcast(BF16)[0:128, dt_ * TSUB:(dt_ + 1) * TSUB]
                        pg.inc(v.tensor_tensor(xpb(dt_, 3 + sec * TSUB, TSUB),
                                               yev, src_z, ALU.mult), VM)

            @block.gpsimd
            def _(g):
                pg.wait_all(g, (VM, SM, TM, DMA_SP))
                for dt_ in range(2):
                    pg.inc(g.dma_start(out=d_gy_pub[dt_ * 128:(dt_ + 1) * 128, :],
                                       in_=xpb(dt_, 3, L)), DMA_G, 16)
                pg.wait(g, DMA_G)
                pg.inc(g.collective_compute("AllGather", ALU.bypass, replica_groups=rg,
                                            ins=[d_gy_pub], outs=[d_gy_all]), CC)

            if dbg:
                @block.gpsimd
                def _(g):
                    pg.wait_all(g, (VM, SM, TM))
                    for dt_ in range(2):
                        pg.inc(g.dma_start(out=dbg["bgy"][dt_ * 128:(dt_ + 1) * 128, :],
                                           in_=xpb(dt_, 3, L)), DMA_G, 16)
                    pg.inc(g.dma_start(out=dbg["bcar"], in_=CARRY[:]), DMA_G, 16)
                    pg.wait(g, DMA_G)

            AR.release()
            gyb_off = AR.alloc(4096)
            wo2_off = AR.alloc(1024)
            GYB = AR.b16(gyb_off, 8192)
            WO2 = AR.b16(wo2_off, 2048)

            for s4 in range(4):
                @block.sync
                def _(sy, s4=s4):
                    pg.wait(sy, CC)
                    pg.wait(sy, TM)
                    for arm in sy.switch_core_id(n=128):
                        kk = arm.logical % W
                        t0 = kk * TLOC + s4 * TSUB
                        for kt in range(16):
                            sy.dma_start(
                                out=GYB[:, kt * TSUB:(kt + 1) * TSUB],
                                in_=d_gy_all[kt // 2, (kt % 2) * 128:(kt % 2) * 128 + 128,
                                             t0:t0 + TSUB]).then_inc(pg._s(DMA_IN)[0], 16)
                    pg.sems[DMA_IN][1] += 16 * 16

                for m in range(8):
                    @block.scalar
                    def _(s, m=m):
                        pg.wait(s, TM)
                        pg.inc(s.dma_start(out=WO2[0:128, 0:2048]
                                           .rearrange("p (k a) -> p k a", a=128),
                                           in_=d_wcast[:, WC_OUT:WC_OUT + 16384]
                                           .rearrange("p (k a) -> p k a", a=1024)
                                           [:, :, m * 128:(m + 1) * 128]), DMA_SP, 16)

                    @block.tensor
                    def _(t, m=m):
                        pg.wait(t, DMA_SP)
                        pg.wait(t, DMA_IN)
                        pg.wait(t, VM)
                        b = banks.take(t)
                        for k in range(16):
                            mm = t.matmul(b[0:128, :],
                                          WO2[0:128, k * 128:(k + 1) * 128],
                                          GYB[:, k * TSUB:(k + 1) * TSUB],
                                          start=(k == 0), stop=(k == 15))
                        pg.inc(mm, TM)

                    bnk = banks.banks[(banks.i - 1) % len(banks.banks)]

                    @block.vector
                    def _(v, m=m, s4=s4, b=bnk):
                        pg.wait(v, TM)
                        cc = RSUBS[s4]
                        xs = X[:, m * TX + cc:m * TX + cc + TSUB]
                        pg.inc(v.tensor_tensor(xs, xs, b[0:128, :], ALU.add), VM)
                        banks.consumed(b, VM, pg.val(VM))
            AR.release()

        for s in st:
            if s == "attn_a":
                emit_attn_a(xcore=("noxc" not in st))
            elif s == "attn_b":
                emit_attn_b()
            elif s in ("ffn1", "ffn2"):
                ffn_stage(s)

        @block.gpsimd
        def _(g):
            pg.wait_all(g, ALLSEMS)
            for f in range(8):
                pg.inc(g.dma_start(out=d_out[f * 128:(f + 1) * 128, :],
                                   in_=X[:, f * TX + HALO:f * TX + TX]), ST, 16)
            pg.wait(g, ST)

    return nc


def kernel(**inputs):
    maps = prep_inputs(inputs)
    nc = build_nc()
    res = run_bass_kernel_spmd(nc, maps, core_ids=list(range(W)))
    outs = [res.results[k]["out_hT"] for k in range(W)]
    return np.concatenate([o.T for o in outs], axis=0).reshape(1, L, H).astype(np.float32)
